# version 09
# baseline (speedup 1.0000x reference)
import numpy as np
from contextlib import ExitStack

import concourse.bass as bass
import concourse.mybir as mybir
from concourse.bass_utils import run_bass_kernel_spmd

F32 = mybir.dt.float32
BF16 = mybir.dt.bfloat16
I8 = mybir.dt.int8
AF = mybir.ActivationFunctionType
ALU = mybir.AluOpType
AXX = mybir.AxisListType.X

B, L, DM = 4, 4096, 2048
DI, NS, TR, KC = 4096, 16, 128, 4
HI, INTER = 2048, 8192
NH, NKV, HD = 16, 4, 128
TC = 2048
T = 512
NP = TC // T
W = 256
NE = DI // 128
NM = DM // 128
NJ = INTER // 128
EPS = 1e-5
KBOUND = 32.0
RSQD = float(1.0 / np.sqrt(128.0))

ENGINES = ("tensor", "vector", "scalar", "gpsimd", "sync")


class Sem:
    __slots__ = ("name", "n", "h")

    def __init__(self, name):
        self.name = name
        self.n = 0
        self.h = None


class Sched:
    def __init__(self):
        self.ops = {e: [] for e in ENGINES}
        self.sems = []

    def sem(self, name):
        s = Sem(name)
        self.sems.append(s)
        return s

    def on(self, engine, fn, waits=(), inc=None, inc_n=1):
        waits = [(s, v) for (s, v) in waits if v > 0]
        for s, v in waits:
            assert v <= s.n, f"wait on {s.name} for {v} > emitted {s.n}"
        self.ops[engine].append((tuple(waits), fn, inc, inc_n))
        if inc is not None:
            inc.n += inc_n
        return inc.n if inc is not None else None

    def dma(self, engine, out, in_, sem, waits=()):
        self.on(engine, lambda eng, o=out, i=in_: eng.dma_start(out=o, in_=i),
                waits=waits, inc=sem, inc_n=16)
        return sem.n

    def realize(self, block):
        for ename in ENGINES:
            items = self.ops[ename]
            if not items:
                continue

            def body(eng, items=items):
                for waits, fn, inc, inc_n in items:
                    for s, v in waits:
                        eng.wait_ge(s.h, v)
                    instr = fn(eng)
                    if inc is not None:
                        instr.then_inc(inc.h, inc_n)
            getattr(block, ename)(body)



def _tiles4(w):
    A, Bc = w.shape
    na, nb = A // 128, Bc // 128
    return np.ascontiguousarray(
        w.reshape(na, 128, nb, 128).transpose(1, 2, 0, 3)).reshape(128, nb * na * 128)


def _tiles4_T(w):
    Bo, Ai = w.shape
    nb, na = Bo // 128, Ai // 128
    return np.ascontiguousarray(
        w.reshape(nb, 128, na, 128).transpose(3, 0, 2, 1)).reshape(128, nb * na * 128)


def make_shards(inputs):
    x = np.asarray(inputs["L0_attn_a_hidden_states"])
    w_in = np.asarray(inputs["L0_attn_a_in_proj_w"])
    w_in_g = _tiles4_T(w_in[DI:, :])
    w_in_h = _tiles4_T(w_in[:DI, :])
    w_in_arr = np.concatenate([w_in_g, w_in_h], axis=1)
    w_out_arr = _tiles4_T(np.asarray(inputs["L0_attn_a_out_proj_w"]))
    xw = np.asarray(inputs["L0_attn_a_x_proj_w"])
    xw_arr = np.ascontiguousarray(xw.T.reshape(NE, 128, 160).transpose(1, 0, 2)).reshape(128, NE * 160)
    dtw = np.asarray(inputs["L0_attn_a_dt_proj_w"])
    dtw_arr = np.ascontiguousarray(dtw.reshape(NE, 128, 128).transpose(2, 0, 1)).reshape(128, NE * 128)
    wq_arr = _tiles4(np.asarray(inputs["L0_attn_b_wq"]))
    wk_arr = _tiles4(np.asarray(inputs["L0_attn_b_wk"]))
    wv_arr = _tiles4(np.asarray(inputs["L0_attn_b_wv"]))
    wo_arr = _tiles4(np.asarray(inputs["L0_attn_b_wo"]))
    wg_arr = _tiles4(np.asarray(inputs["L0_ffn_w_gate"]))
    wu_arr = _tiles4(np.asarray(inputs["L0_ffn_w_up"]))
    wd_arr = _tiles4(np.asarray(inputs["L0_ffn_w_down"]))

    def col_d(v):
        return np.ascontiguousarray(np.asarray(v).reshape(-1, 128).T)

    A_log = np.asarray(inputs["L0_attn_a_A_log"])
    a_arr = np.ascontiguousarray(A_log.reshape(NE, 128, NS).transpose(1, 0, 2)).reshape(128, NE * NS)
    convw = np.asarray(inputs["L0_attn_a_conv_w"])
    convw_arr = np.ascontiguousarray(convw.reshape(NE, 128, KC).transpose(1, 0, 2)).reshape(128, NE * KC)
    consts = np.concatenate([
        a_arr,
        convw_arr,
        col_d(inputs["L0_attn_a_conv_b"]),
        col_d(inputs["L0_attn_a_dt_proj_b"]),
        col_d(inputs["L0_attn_a_D"]),
        col_d(inputs["L0_ffn_s_gate"]),
        col_d(inputs["L0_ffn_s_up"]),
        col_d(inputs["L0_ffn_s_down"]),
        col_d(inputs["L0_attn_b_bq"]),
        col_d(inputs["L0_attn_b_bk"]),
        col_d(inputs["L0_attn_b_bv"]),
        col_d(inputs["L0_attn_b_bo"]),
        np.full((128, 1), EPS, np.float32),
        np.full((128, 1), RSQD, np.float32),
        np.full((128, 1), KBOUND * np.sqrt(128.0), np.float32),
        np.full((128, 1), 0.0, np.float32),
    ], axis=1).astype(np.float32)

    shared = dict(w_in=w_in_arr, w_out=w_out_arr, xw=xw_arr.astype(np.float32),
                  dtw=dtw_arr.astype(np.float32), wq=wq_arr, wk=wk_arr, wv=wv_arr,
                  wo=wo_arr, wg=wg_arr, wu=wu_arr, wd=wd_arr, consts=consts)

    in_maps = []
    for c in range(8):
        s, half = c // 2, c % 2
        t0 = half * TC
        xs = x[s]
        main = xs[t0:t0 + TC]
        if half == 0:
            halo = np.zeros((3, DM), np.float32)
            warm = np.zeros((W, DM), np.float32)
            wmask = np.zeros((128, 1), np.float32)
        else:
            halo = xs[t0 - W - 3:t0 - W]
            warm = xs[t0 - W:t0]
            wmask = np.ones((128, 1), np.float32)
        xin_fm = np.ascontiguousarray(np.concatenate([halo, warm, main], axis=0).T)
        m = dict(shared)
        m["x_fm"] = xin_fm
        m["wmask"] = wmask
        in_maps.append(m)
    return in_maps



def build_program(upto="full", debug=False):
    nc = bass.Bass("TRN2", target_bir_lowering=False, debug=False)
    S = Sched()
    ap = lambda t: t.ap()

    x_fm = ap(nc.dram_tensor("x_fm", [DM, 3 + W + TC], F32, kind="ExternalInput"))
    wmask = ap(nc.dram_tensor("wmask", [128, 1], F32, kind="ExternalInput"))
    consts_in = ap(nc.dram_tensor("consts", [128, 924], F32, kind="ExternalInput"))
    w_in = ap(nc.dram_tensor("w_in", [128, 64 * 16 * 128], F32, kind="ExternalInput"))
    w_out = ap(nc.dram_tensor("w_out", [128, 16 * 32 * 128], F32, kind="ExternalInput"))
    xw_in = ap(nc.dram_tensor("xw", [128, NE * 160], F32, kind="ExternalInput"))
    dtw_in = ap(nc.dram_tensor("dtw", [128, NE * 128], F32, kind="ExternalInput"))
    wq_in = ap(nc.dram_tensor("wq", [128, 16 * 16 * 128], F32, kind="ExternalInput"))
    wk_in = ap(nc.dram_tensor("wk", [128, 4 * 16 * 128], F32, kind="ExternalInput"))
    wv_in = ap(nc.dram_tensor("wv", [128, 4 * 16 * 128], F32, kind="ExternalInput"))
    wo_in = ap(nc.dram_tensor("wo", [128, 16 * 16 * 128], F32, kind="ExternalInput"))
    wg_in = ap(nc.dram_tensor("wg", [128, 64 * 16 * 128], I8, kind="ExternalInput"))
    wu_in = ap(nc.dram_tensor("wu", [128, 64 * 16 * 128], I8, kind="ExternalInput"))
    wd_in = ap(nc.dram_tensor("wd", [128, 16 * 64 * 128], I8, kind="ExternalInput"))

    out = ap(nc.dram_tensor("out", [DM, TC], F32, kind="ExternalOutput"))

    wib = ap(nc.dram_tensor("wib", [128, 64 * 16 * 128], BF16, kind="Internal"))
    dtwb = ap(nc.dram_tensor("dtwb", [128, 32 * 128], BF16, kind="Internal"))
    xwb = ap(nc.dram_tensor("xwb", [128, 32 * 160], BF16, kind="Internal"))
    wob = ap(nc.dram_tensor("wob", [128, 16 * 32 * 128], BF16, kind="Internal"))
    wqb = ap(nc.dram_tensor("wqb", [128, 16 * 16 * 128], BF16, kind="Internal"))
    wkb = ap(nc.dram_tensor("wkb", [128, 4 * 16 * 128], BF16, kind="Internal"))
    wvb = ap(nc.dram_tensor("wvb", [128, 4 * 16 * 128], BF16, kind="Internal"))
    wob2 = ap(nc.dram_tensor("wob2", [128, 16 * 16 * 128], BF16, kind="Internal"))
    x_d = ap(nc.dram_tensor("x_d", [NP, NM, 128, T], F32, kind="Internal"))
    q_d = ap(nc.dram_tensor("q_d", [NP, NM, 128, T], BF16, kind="Internal"))
    cc_i = ap(nc.dram_tensor("cc_i", [1024, TC], BF16, kind="Internal"))
    cc_o = ap(nc.dram_tensor("cc_o", [2048, TC], BF16, kind="Internal"))
    pv_d = ap(nc.dram_tensor("pv_d", [NP, NM, 128, T], BF16, kind="Internal"))

    dbg = {}
    if debug:
        dbg["x1"] = ap(nc.dram_tensor("dbg_x1", [NP, NM, 128, T], F32, kind="ExternalOutput"))
        dbg["xs"] = ap(nc.dram_tensor("dbg_xs", [NE, 128, T], F32, kind="ExternalOutput"))
        dbg["dt"] = ap(nc.dram_tensor("dbg_dt", [NE, 128, T], F32, kind="ExternalOutput"))
        dbg["ssm"] = ap(nc.dram_tensor("dbg_ssm", [160, T], F32, kind="ExternalOutput"))
        dbg["y"] = ap(nc.dram_tensor("dbg_y", [NE, 128, T], F32, kind="ExternalOutput"))
        dbg["tail"] = ap(nc.dram_tensor("dbg_tail", [2, 128, NE * 3], F32, kind="ExternalOutput"))
        dbg["lnh"] = ap(nc.dram_tensor("dbg_lnh", [128, 48], F32, kind="ExternalOutput"))
        dbg["sg"] = ap(nc.dram_tensor("dbg_sg", [NE, 128, T], F32, kind="ExternalOutput"))
        dbg["yg"] = ap(nc.dram_tensor("dbg_yg", [NE, 128, T], F32, kind="ExternalOutput"))
        dbg["yd"] = ap(nc.dram_tensor("dbg_yd", [NE, 128, T], F32, kind="ExternalOutput"))
        dbg["wln"] = ap(nc.dram_tensor("dbg_wln", [NM, 128, T], F32, kind="ExternalOutput"))
        dbg["wxs"] = ap(nc.dram_tensor("dbg_wxs", [NE, 128, T], F32, kind="ExternalOutput"))

    st = ExitStack()
    e = st.enter_context

    A1 = e(nc.sbuf_tensor("A1", [128, 32768], BF16))
    A2 = e(nc.sbuf_tensor("A2", [128, 16384], BF16))
    A3 = e(nc.sbuf_tensor("A3", [128, 8192], BF16))
    A4 = e(nc.sbuf_tensor("A4", [128, 8192], F32))
    SCR = e(nc.sbuf_tensor("SCR", [128, 5120], BF16))
    DTB = e(nc.sbuf_tensor("DTB", [128, 1536], F32))
    A5 = e(nc.sbuf_tensor("A5", [128, 2048], F32))
    WS = e(nc.sbuf_tensor("WS", [128, 8192], BF16))
    CT = e(nc.sbuf_tensor("CT", [128, 1024], F32))
    TSB = e(nc.sbuf_tensor("TSB", [128, 512], BF16))
    BCF = e(nc.sbuf_tensor("BCF", [32, 512], BF16))
    SELS = e(nc.sbuf_tensor("SELS", [32, 32 * 128], BF16))
    HCA = e(nc.sbuf_tensor("HCA", [128, 512], F32))
    HSS = e(nc.sbuf_tensor("HSS", [128, 2048], BF16))
    DTW = e(nc.sbuf_tensor("DTW", [128, 2 * 128], BF16))
    XWS = e(nc.sbuf_tensor("XWS", [128, 2 * 160], BF16))
    ONE = e(nc.sbuf_tensor("ONE", [128, 128], BF16))
    IDB = e(nc.sbuf_tensor("IDB", [128, 128], BF16))
    MSK = e(nc.sbuf_tensor("MSK", [128, 1], F32))
    TAIL = e(nc.sbuf_tensor("TAIL", [128, NE * 3], F32))

    PS = [e(nc.psum_tensor(f"PS{i}", [128, 512], F32)) for i in range(7)]
    PSB = e(nc.psum_tensor("PSB", [128, 1024], BF16))

    O_A, O_CW, O_CB, O_DTB, O_D = 0, 512, 640, 672, 704
    O_SG, O_SU, O_SD, O_BQ, O_BK, O_BV, O_BO = 736, 800, 864, 880, 896, 900, 904
    O_EPS, O_RSQ, O_KB = 920, 921, 922

    ld = S.sem("ld")
    lg = S.sem("lg")
    pe = S.sem("pe")
    ve = S.sem("ve")
    se = S.sem("se")
    po = S.sem("po")
    sx = S.sem("sx")
    sg_ = S.sem("sg")
    cc = S.sem("cc")
    pc = S.sem("pc")

    class Slots:
        def __init__(self, n=4, width=2048):
            self.n, self.w = n, width
            self.consumed_pe = [0] * n
            self.rr = 0

        def get(self, k=1):
            s0 = self.rr
            if k == 2:
                if s0 % 2:
                    s0 = (s0 + 1) % self.n
                if s0 + 1 >= self.n:
                    s0 = 0
            self.rr = (s0 + k) % self.n
            waits = []
            for s in range(s0, s0 + k):
                if self.consumed_pe[s] > 0:
                    waits.append((pe, self.consumed_pe[s]))
            ap_ = WS[:, s0 * self.w:(s0 + k) * self.w]
            return (s0, k, ap_, waits)

        def done(self, s0, k, pe_count):
            for s in range(s0, s0 + k):
                self.consumed_pe[s] = pe_count
    SL = Slots()

    S.dma("sync", CT[:, 0:924], consts_in[:, :], ld)
    S.dma("sync", MSK[:, :], wmask[:, :], ld)
    ld_consts = ld.n
    S.on("scalar", lambda eng: eng.activation(CT[:, 0:512], CT[:, 0:512], AF.Exp),
         waits=[(ld, ld_consts)], inc=se)
    S.on("vector", lambda eng: eng.tensor_scalar_mul(CT[:, 0:512], CT[:, 0:512], -1.0),
         waits=[(se, se.n)], inc=ve)
    ve_consts = ve.n
    S.on("vector", lambda eng: eng.memset(ONE[:, :], 1.0), inc=ve)
    S.on("vector", lambda eng: eng.memset(HCA[:, :], 0.0), inc=ve)
    S.on("vector", lambda eng: eng.memset(IDB[:, :], 0.0), inc=ve)
    ve_set = ve.n
    S.on("gpsimd", lambda eng: eng.affine_select(out=IDB[:, :], in_=IDB[:, :], pattern=[[-1, 128]],
                                                 compare_op=ALU.not_equal, fill=1.0, base=0,
                                                 channel_multiplier=1),
         waits=[(ve, ve_set)], inc=po)
    S.on("vector", lambda eng: eng.memset(SELS[:, :], 0.0), inc=ve)
    ve_sel = ve.n
    S.on("gpsimd", lambda eng: eng.affine_select(out=SELS[:, :].rearrange("k (n m) -> k n m", n=32),
                                                 in_=SELS[:, :].rearrange("k (n m) -> k n m", n=32),
                                                 pattern=[[-1, 32], [0, 128]],
                                                 compare_op=ALU.not_equal, fill=1.0, base=0,
                                                 channel_multiplier=1),
         waits=[(ve, ve_sel)], inc=po)
    po_idn = po.n

    def precast(dst, src, total):
        n = 0
        while n < total:
            m = min(16384, total - n)
            S.dma("gpsimd", dst[:, n:n + m], src[:, n:n + m], pc)
            n += m
        return pc.n
    lg_wib = precast(wib, w_in, 64 * 16 * 128)
    precast(dtwb, dtw_in, 32 * 128)
    lg_dtxw = precast(xwb, xw_in, 32 * 160)
    lg_wob = precast(wob, w_out, 16 * 32 * 128)
    precast(wqb, wq_in, 16 * 16 * 128)
    precast(wkb, wk_in, 4 * 16 * 128)
    lg_wkv = precast(wvb, wv_in, 4 * 16 * 128)
    lg_wo2 = precast(wob2, wo_in, 16 * 16 * 128)

    def emit_ln(src_tile, dst_tile, w, pre_waits=(), nt=NM, sq_scratch=None,
                src_bf16=False, cast_scratch=None):
        for mb in range(nt):
            waits = list(pre_waits) if mb == 0 else []
            if mb >= 2:
                waits.append((pe, pe.n))
            if src_bf16:
                mv = src_tile(mb)
                sq_ap = sq_scratch(mb % 2)
                S.on("vector", lambda eng, d=mv, q=sq_ap: eng.tensor_tensor(q, d, d, ALU.mult),
                     waits=waits, inc=ve)
            else:
                mv = cast_scratch(mb % 2)
                sq_ap = sq_scratch(mb % 2)
                S.on("vector", lambda eng, s=src_tile(mb), d=mv: eng.tensor_copy(d, s),
                     waits=waits, inc=ve)
                S.on("vector", lambda eng, d=mv, q=sq_ap: eng.tensor_tensor(q, d, d, ALU.mult),
                     inc=ve)
            vq = ve.n
            S.on("tensor", lambda eng, d=mv, mb=mb, w=w: eng.matmul(
                PS[4][:, 0:w], ONE[:, :], d, start=(mb == 0), stop=(mb == nt - 1)),
                waits=[(ve, vq)], inc=pe)
            S.on("tensor", lambda eng, q=sq_ap, mb=mb, w=w: eng.matmul(
                PS[5][:, 0:w], ONE[:, :], q, start=(mb == 0), stop=(mb == nt - 1)),
                inc=pe)
        pe_stats = pe.n
        ww = min(w, 512)
        MU = A5[:, 0:ww][:, 0:w]
        RS = A5[:, 512:512 + w]
        VR = A5[:, 1024:1024 + w]
        TMP = A5[:, 1536:1536 + w]
        S.on("vector", lambda eng, w=w: eng.tensor_scalar_mul(MU, PS[4][:, 0:w], 1.0 / DM),
             waits=[(pe, pe_stats)], inc=ve)
        S.on("vector", lambda eng: eng.scalar_tensor_tensor(VR, MU, -1.0, MU, ALU.mult, ALU.mult),
             inc=ve)
        S.on("vector", lambda eng, w=w: eng.scalar_tensor_tensor(VR, PS[5][:, 0:w], 1.0 / DM, VR,
                                                                 ALU.mult, ALU.add), inc=ve)
        ve_var = ve.n
        S.on("scalar", lambda eng: eng.activation(VR, VR, AF.Sqrt,
                                                  bias=CT[:, O_EPS:O_EPS + 1]),
             waits=[(ve, ve_var)], inc=se)
        S.on("vector", lambda eng: eng.reciprocal(RS, VR), waits=[(se, se.n)], inc=ve)
        for mb in range(nt):
            S.on("vector", lambda eng, s=src_tile(mb): eng.tensor_tensor(TMP, s, MU, ALU.subtract),
                 inc=ve)
            S.on("vector", lambda eng, d=dst_tile(mb): eng.tensor_tensor(d, TMP, RS, ALU.mult),
                 inc=ve)
        return ve.n

    def XS(i):
        return A1[:, i * T:(i + 1) * T]

    def SG(i):
        return A1[:, 16384 + i * T:16384 + (i + 1) * T]

    def ln_src(mb):
        return A3[:, mb * T:(mb + 1) * T]

    def ln_sq(par):
        return A2[:, 8192 + par * T:8192 + (par + 1) * T]

    def emit_halo_tails():
        for mb in range(NM):
            S.dma("gpsimd", ln_src(mb), x_fm[mb * 128:(mb + 1) * 128, 0:T], lg)
        lw = lg.n
        lnw = emit_ln(ln_src, lambda mb: A2[:, mb * T:(mb + 1) * T], T,
                      pre_waits=[(lg, lw), (ve, ve_consts), (po, po_idn)],
                      sq_scratch=ln_sq, src_bf16=True)
        for i in range(NE):
            s0, k, wap, swaits = SL.get(1)
            nw = S.dma("gpsimd", wap, wib[:, (32 + i) * 2048:(33 + i) * 2048], lg,
                       waits=swaits + ([(pc, lg_wib)] if i == 0 else []))
            for hb in range(NM):
                S.on("tensor", lambda eng, hb=hb, wap=wap: eng.matmul(
                    PS[6][:, 0:3], wap[:, hb * 128:(hb + 1) * 128],
                    A2[:, hb * T:hb * T + 3], start=(hb == 0), stop=(hb == NM - 1)),
                    waits=[(lg, nw), (ve, max(lnw, ve.n))] if hb == 0 else (),
                    inc=pe if hb == NM - 1 else None)
            SL.done(s0, k, pe.n)
            S.on("vector", lambda eng, i=i: eng.tensor_copy(TAIL[:, i * 3:(i + 1) * 3],
                                                            PS[6][:, 0:3]),
                 waits=[(pe, pe.n)], inc=ve)

    def mamba_head(p, wd=T):
        warm = (p < 0)
        st0 = mamba_pass.state
        col0 = 3 + (0 if warm else W + T * p)
        for mb in range(NM):
            S.dma("gpsimd", ln_src(mb)[:, 0:wd],
                  x_fm[mb * 128:(mb + 1) * 128, col0:col0 + wd], lg,
                  waits=[(ve, st0["ve_y_prev"]), (po, st0["po_endscan"])] if mb == 0 else ())
        st0["lx"] = lg.n

    def mamba_pass(p, wd=T):
        warm = (p < 0)
        st0 = mamba_pass.state
        lx = st0["lx"]
        lnw = emit_ln(lambda mb: ln_src(mb)[:, 0:wd],
                      lambda mb: A2[:, mb * T:mb * T + wd], wd,
                      pre_waits=[(lg, lx), (ve, ve_consts), (po, po.n)],
                      sq_scratch=lambda par: ln_sq(par)[:, 0:wd], src_bf16=True)
        if debug and warm:
            for mb in range(NM):
                S.dma("gpsimd", dbg["wln"][mb], A2[:, mb * T:(mb + 1) * T], sg_,
                      waits=[(ve, lnw)])

        def etile(idx_w, out_emit):
            s0, k, wap, swaits = SL.get(1)
            nw = S.dma("gpsimd", wap, wib[:, idx_w * 2048:(idx_w + 1) * 2048], lg,
                       waits=swaits)
            ps = PS[etile.cnt % 2]
            etile.cnt += 1
            for hb in range(NM):
                S.on("tensor", lambda eng, hb=hb, wap=wap, ps=ps: eng.matmul(
                    ps[:, 0:wd], wap[:, hb * 128:(hb + 1) * 128],
                    A2[:, hb * T:hb * T + wd], start=(hb == 0), stop=(hb == NM - 1)),
                    waits=[(lg, nw), (ve, max(lnw, ve.n)), (se, se.n)] if hb == 0 else (),
                    inc=pe if hb == NM - 1 else None)
            SL.done(s0, k, pe.n)
            out_emit(ps, pe.n)
        etile.cnt = 0

        if not warm:
            for i in range(NE):
                def emit_sg(ps, pen, i=i):
                    S.on("scalar", lambda eng: eng.activation(SG(i), ps[:, :], AF.Silu),
                         waits=[(pe, pen)], inc=se)
                etile(i, emit_sg)

        for i in range(NE):
            def emit_conv(ps, pen, i=i):
                HSA = A5[:, 0:wd + 3]
                ACC = A5[:, 515 + (i % 2) * 512:515 + (i % 2) * 512 + wd]
                S.on("vector", lambda eng: eng.tensor_copy(HSA[:, 0:3], TAIL[:, i * 3:(i + 1) * 3]),
                     waits=[(pe, pen)], inc=ve)
                S.on("vector", lambda eng: eng.tensor_copy(HSA[:, 3:3 + wd], ps[:, 0:wd]), inc=ve)
                S.on("vector", lambda eng: eng.tensor_copy(TAIL[:, i * 3:(i + 1) * 3],
                                                           ps[:, wd - 3:wd]), inc=ve)
                ve_hsa = ve.n
                acc_se = mamba_pass.state["acc_se"]
                S.on("vector", lambda eng, ACC=ACC: eng.tensor_scalar(
                    ACC, HSA[:, 0:wd], CT[:, O_CW + i * 4:O_CW + i * 4 + 1], None, ALU.mult),
                    waits=[(ve, ve_hsa)] + ([(se, acc_se[i % 2])]
                                            if acc_se[i % 2] else []), inc=ve)
                for kk in range(1, KC):
                    S.on("vector", lambda eng, kk=kk, ACC=ACC: eng.scalar_tensor_tensor(
                        ACC, HSA[:, kk:kk + wd], CT[:, O_CW + i * 4 + kk:O_CW + i * 4 + kk + 1],
                        ACC, ALU.mult, ALU.add), inc=ve)
                ve_acc = ve.n
                S.on("scalar", lambda eng, ACC=ACC: eng.activation(
                    XS(i)[:, 0:wd], ACC, AF.Silu, bias=CT[:, O_CB + i:O_CB + i + 1]),
                    waits=[(ve, ve_acc)], inc=se)
                mamba_pass.state["acc_se"][i % 2] = se.n
                if debug and warm:
                    S.dma("gpsimd", dbg["wxs"][i], XS(i), sg_, waits=[(se, se.n)])
            etile(32 + i, emit_conv)
        se_xs = se.n

        xw_lg = [0] * NE
        xw_lg[0] = S.dma("sync", XWS[:, 0:160], xwb[:, 0:160], ld,
                         waits=[(pc, lg_dtxw)])
        xw_lg[1] = S.dma("sync", XWS[:, 160:320], xwb[:, 160:320], ld)
        for db in range(NE):
            sl = (db % 2) * 160
            S.on("tensor", lambda eng, db=db, sl=sl: eng.matmul(
                PS[2][:, 0:wd], XWS[:, sl:sl + 128], XS(db)[:, 0:wd],
                start=(db == 0), stop=(db == NE - 1)),
                waits=[(ld, xw_lg[db])] + ([(se, se_xs), (ve, ve.n)] if db == 0 else []),
                inc=pe)
            S.on("tensor", lambda eng, db=db, sl=sl: eng.matmul(
                PS[3][0:32, 0:wd], XWS[:, sl + 128:sl + 160], XS(db)[:, 0:wd],
                start=(db == 0), stop=(db == NE - 1)),
                inc=pe)
            if db + 2 < NE:
                xw_lg[db + 2] = S.dma(
                    "sync", XWS[:, sl:sl + 160],
                    xwb[:, (db + 2) * 160:(db + 3) * 160], ld,
                    waits=[(pe, pe.n)])
        pe_xp = pe.n
        S.on("vector", lambda eng: eng.tensor_copy(TSB[:, 0:wd], PS[2][:, 0:wd]),
             waits=[(pe, pe_xp)], inc=ve)
        S.on("vector", lambda eng: eng.tensor_copy(BCF[:, 0:wd], PS[3][0:32, 0:wd]), inc=ve)
        ve_ts = ve.n
        if debug and p == 0:
            S.dma("gpsimd", dbg["ssm"][0:128, :], TSB[:, :], sg_, waits=[(ve, ve_ts)])
            S.dma("gpsimd", dbg["ssm"][128:160, :], BCF[:, :], sg_)

        for n in range(32):
            S.on("tensor", lambda eng, n=n: eng.matmul(
                PS[2 + (n % 2)][:, 0:wd], SELS[:, n * 128:(n + 1) * 128], BCF[:, 0:wd],
                start=True, stop=True),
                waits=[(ve, max(ve_ts, ve.n))] if n < 2 else [(ve, ve.n)],
                inc=pe)
            S.on("vector", lambda eng, n=n: eng.tensor_copy(
                A2[:, n * T:n * T + wd], PS[2 + (n % 2)][:, 0:wd]),
                waits=[(pe, pe.n)], inc=ve)
        ve_bc = ve.n

        st_ = mamba_pass.state
        NG = 14
        dt_lg = [0] * NE
        dt_pe = [0] * NE
        for i in range(NE):
            if i == 0:
                dt_lg[0] = S.dma("sync", DTW[:, 0:128], dtwb[:, 0:128], ld,
                                 waits=[(pe, st_["dtw_pe"]), (pc, lg_dtxw)])
                dt_lg[1] = S.dma("sync", DTW[:, 128:256], dtwb[:, 128:256], ld)
            S.on("tensor", lambda eng, i=i: eng.matmul(
                PS[6][:, 0:wd], DTW[:, (i % 2) * 128:(i % 2) * 128 + 128], TSB[:, 0:wd],
                start=True, stop=True),
                waits=[(ld, dt_lg[i])]
                + ([(ve, ve_bc), (se, se_xs)] if i == 0 else [(se, st_["dti"])]), inc=pe)
            pe_dt = pe.n
            dt_pe[i] = pe_dt
            if i + 2 < NE:
                dt_lg[i + 2] = S.dma(
                    "sync", DTW[:, (i % 2) * 128:(i % 2) * 128 + 128],
                    dtwb[:, (i + 2) * 128:(i + 3) * 128], ld, waits=[(pe, pe_dt)])
            else:
                st_["dtw_pe"] = pe_dt
            DTI = DTB[:, (i % 2) * 512:(i % 2) * 512 + wd]
            DTX = SCR[:, 4096 + (i % 2) * 512:4096 + (i % 2) * 512 + wd]
            E1 = DTB[:, 1024:1024 + wd]
            S.on("scalar", lambda eng, i=i, E1=E1: eng.activation(
                E1, PS[6][:, 0:wd], AF.Exp, bias=CT[:, O_DTB + i:O_DTB + i + 1]),
                waits=[(pe, pe_dt), (ve, st_["dtx"])]
                + ([(sx, sx.n)] if debug else []), inc=se)
            S.on("scalar", lambda eng, DTI=DTI, E1=E1: eng.activation(DTI, E1, AF.Ln, bias=1.0),
                 inc=se)
            st_["dti"] = se.n
            if debug and p == 0:
                S.dma("sync", dbg["dt"][i, :, :], DTI, sx, waits=[(se, st_["dti"])])
            S.on("vector", lambda eng, i=i, DTX=DTX, DTI=DTI: eng.tensor_tensor(
                DTX, DTI, XS(i)[:, 0:wd], ALU.mult),
                waits=[(se, st_["dti"])], inc=ve)
            if warm:
                S.on("vector", lambda eng, DTX=DTX: eng.tensor_scalar(
                    DTX, DTX, MSK[:, 0:1], None, ALU.mult), inc=ve)
            st_["dtx"] = ve.n
            for n in range(NG, NS):
                g = n - NG
                UBG = SCR[:, 3072 + g * 512:3072 + g * 512 + wd]
                S.on("gpsimd", lambda eng, n=n, UBG=UBG, DTX=DTX: eng.tensor_tensor(
                    UBG, DTX, A2[:, n * T:n * T + wd], ALU.mult),
                    waits=[(ve, st_["dtx"]), (ve, st_["ubg_free"][g])], inc=po)
                st_["ub_g"][g] = po.n
            po_m7 = 0
            for n in range(NS):
                j = i * NS + n
                sda = j % 4
                DA = SCR[:, sda * 512:sda * 512 + wd]
                if n < NG:
                    UB = SCR[:, 2048 + (j % 2) * 512:2048 + (j % 2) * 512 + wd]
                else:
                    UB = SCR[:, 3072 + (n - NG) * 512:3072 + (n - NG) * 512 + wd]
                HS_ = HSS[:, sda * 512:sda * 512 + wd]
                S.on("scalar", lambda eng, j=j, DA=DA, DTI=DTI: eng.activation(
                    DA, DTI, AF.Exp, scale=CT[:, O_A + j:O_A + j + 1]),
                    waits=[(ve, st_["da_free"][sda])], inc=se)
                se_da = se.n
                if n < NG:
                    S.on("vector", lambda eng, n=n, UB=UB, DTX=DTX: eng.tensor_tensor(
                        UB, DTX, A2[:, n * T:n * T + wd], ALU.mult), inc=ve)
                sw = [(se, se_da)]
                if n >= NG:
                    sw.append((po, st_["ub_g"][n - NG]))
                if not warm:
                    sw.append((po, st_["hs_free"][sda]))
                S.on("vector", lambda eng, j=j, HS_=HS_, DA=DA, UB=UB: eng.tensor_tensor_scan(
                    HS_, DA, UB, HCA[:, j:j + 1], ALU.mult, ALU.add),
                    waits=sw, inc=ve)
                ve_scan = ve.n
                st_["da_free"][sda] = ve_scan
                st_["scan_ve"][sda] = ve_scan
                if n < NG and n % 2 == 0:
                    st_["ubv0_scan"] = ve_scan
                if n >= NG:
                    st_["ubg_free"][n - NG] = ve_scan
                if n >= 3:
                    jc = j - 3
                    S.on("scalar", lambda eng, jc=jc: eng.copy(
                        HCA[:, jc:jc + 1],
                        HSS[:, (jc % 4) * 512 + wd - 1:(jc % 4) * 512 + wd]),
                        waits=[(ve, st_["scan_ve"][jc % 4])], inc=se)
                if not warm:
                    S.on("gpsimd", lambda eng, n=n, HS_=HS_: eng.tensor_tensor(
                        A3[:, n * T:(n + 1) * T], HS_, A2[:, (16 + n) * T:(17 + n) * T], ALU.mult),
                        waits=[(ve, ve_scan)], inc=po)
                    st_["hs_free"][sda] = po.n
                    if n == 7:
                        po_m7 = po.n
            for jc in range(i * NS + NS - 3, i * NS + NS):
                S.on("scalar", lambda eng, jc=jc: eng.copy(
                    HCA[:, jc:jc + 1],
                    HSS[:, (jc % 4) * 512 + wd - 1:(jc % 4) * 512 + wd]),
                    waits=[(ve, st_["scan_ve"][jc % 4])], inc=se)
            if warm:
                continue
            for (a, b) in ((8, 12), (9, 13), (10, 14), (11, 15), (8, 10), (9, 11), (8, 9)):
                S.on("gpsimd", lambda eng, a=a, b=b: eng.tensor_tensor(
                    A3[:, a * T:(a + 1) * T], A3[:, a * T:(a + 1) * T],
                    A3[:, b * T:(b + 1) * T], ALU.add), inc=po)
            po_tree = po.n
            YI = A5[:, 0:512]
            hv = A3[:, 0:8 * T].rearrange("p (n t) -> p t n", n=8)
            S.on("vector", lambda eng, YI=YI, hv=hv: eng.tensor_reduce(YI, hv, AXX, ALU.add),
                 waits=[(po, po_m7)], inc=ve)
            S.on("vector", lambda eng, YI=YI: eng.tensor_tensor(
                YI, YI, A3[:, 8 * T:9 * T], ALU.add), waits=[(po, po_tree)], inc=ve)
            dwait = []
            if debug and p == 0:
                S.dma("sync", dbg["y"][i, :, :], YI, sx, waits=[(ve, ve.n)])
                dwait = [(sx, sx.n)]
            S.on("vector", lambda eng, i=i: eng.scalar_tensor_tensor(
                YI, XS(i), CT[:, O_D + i:O_D + i + 1], YI, ALU.mult, ALU.add),
                waits=dwait, inc=ve)
            dwait = []
            if debug and p == 0:
                S.dma("sync", dbg["yd"][i, :, :], YI, sx, waits=[(ve, ve.n)])
                S.dma("gpsimd", dbg["xs"][i, :, :], XS(i), sg_, waits=[(ve, ve.n)])
                S.dma("gpsimd", dbg["sg"][i, :, :], SG(i), sg_)
                dwait = [(sg_, sg_.n), (sx, sx.n)]
            S.on("vector", lambda eng, i=i: eng.tensor_tensor(XS(i), YI, SG(i), ALU.mult),
                 waits=dwait, inc=ve)
            if debug and p == 0:
                S.dma("gpsimd", dbg["yg"][i, :, :], XS(i), sg_, waits=[(ve, ve.n)])
        st_["ve_y_prev"] = ve.n
        st_["po_endscan"] = po.n
        st_["ve_y"] = ve.n

    def mamba_out(p):
        st0 = mamba_pass.state
        ve_y = st0["ve_y"]
        col0 = 3 + W + T * p
        for mb in range(NM):
            s0, k, wap, swaits = SL.get(2)
            nw = S.dma("gpsimd", wap, wob[:, mb * 4096:(mb + 1) * 4096], lg,
                       waits=swaits + ([(pc, lg_wob)] if mb == 0 else []))
            ps = PS[mb % 2]
            for db in range(NE):
                S.on("tensor", lambda eng, db=db, wap=wap, ps=ps: eng.matmul(
                    ps[:, :], wap[:, db * 128:(db + 1) * 128], XS(db),
                    start=(db == 0), stop=(db == NE - 1)),
                    waits=[(lg, nw), (ve, max(ve_y, ve.n))] if db == 0 else (),
                    inc=pe if db == NE - 1 else None)
            SL.done(s0, k, pe.n)
            pe_o = pe.n
            xa = A4[:, 6144 + (mb % 4) * T:6144 + (mb % 4) * T + T]
            lxr = S.dma("sync", xa, x_fm[mb * 128:(mb + 1) * 128, col0:col0 + T], ld,
                        waits=([(ve, ve.n), (po, po.n)] if mb < 4 else []) + [(sx, sx.n)])
            S.on("vector", lambda eng, xa=xa, ps=ps: eng.tensor_tensor(xa, xa, ps[:, :], ALU.add),
                 waits=[(pe, pe_o), (ld, lxr)], inc=ve)
            ve_x1 = ve.n
            S.dma("sync", x_d[p, mb, :, :], xa, sx, waits=[(ve, ve_x1)])
            if debug:
                S.dma("sync", dbg["x1"][p, mb, :, :], xa, sx)

    mamba_pass.state = {"dti": 0, "dtx": 0, "da_free": [0, 0, 0, 0],
                        "hs_free": [0, 0, 0, 0], "ubg_free": [0, 0],
                        "ub_g": [0, 0], "acc_se": [0, 0],
                        "scan_ve": [0, 0, 0, 0], "ubv0_scan": 0, "dtw_pe": 0,
                        "ve_y_prev": 0, "po_endscan": 0, "ve_y": 0, "lx": 0}
    emit_halo_tails()
    if debug:
        S.dma("sync", dbg["tail"][0], TAIL[:, :], sx, waits=[(ve, ve.n)])
        S.dma("gpsimd", dbg["lnh"][:, :].rearrange("p (mb j) -> p mb j", j=3), A2[:, 0:NM * T].rearrange("p (mb t) -> p mb t", mb=NM)[:, :, 0:3], sg_, waits=[(ve, ve.n)])
    mamba_pass.state["ve_y_prev"] = ve.n
    mamba_pass.state["po_endscan"] = po.n
    mamba_head(-1, wd=W)
    mamba_pass(-1, wd=W)
    if debug:
        S.dma("sync", dbg["tail"][1], TAIL[:, :], sx, waits=[(ve, ve.n)])
    for p in range(NP):
        mamba_head(p)
        if p > 0:
            mamba_out(p - 1)
        mamba_pass(p)
    mamba_out(NP - 1)

    def A3s(k):
        return A3[:, k * T:(k + 1) * T]

    def load_chunk_A4(dram4, c, extra=()):
        return S.dma("sync", A4[:, 0:NM * T].rearrange("p (mb t) -> p mb t", mb=NM),
                     dram4[c].transpose([1, 0, 2]), ld, waits=extra)

    def emit_ffn_gud(lnoff, f_prewait=()):
        res = []
        for jb in range(NJ):
            sg0, kg, wgap, wg_w = SL.get(1)
            ng = S.dma("gpsimd", wgap, wg_in[:, jb * 2048:(jb + 1) * 2048], lg, waits=wg_w)
            su0, ku, wuap, wu_w = SL.get(1)
            nu = S.dma("gpsimd", wuap, wu_in[:, jb * 2048:(jb + 1) * 2048], lg, waits=wu_w)
            psg = PS[jb % 2]
            psu = PS[2 + jb % 2]
            for hb in range(NM):
                S.on("tensor", lambda eng, hb=hb, wgap=wgap, psg=psg, lnoff=lnoff: eng.matmul(
                    psg[:, :], wgap[:, hb * 128:(hb + 1) * 128],
                    A2[:, lnoff + hb * T:lnoff + (hb + 1) * T],
                    start=(hb == 0), stop=(hb == NM - 1)),
                    waits=[(lg, ng), (se, se.n), (ve, ve.n)] if hb == 0 else (),
                    inc=pe if hb == NM - 1 else None)
            SL.done(sg0, kg, pe.n)
            pe_g = pe.n
            for hb in range(NM):
                S.on("tensor", lambda eng, hb=hb, wuap=wuap, psu=psu, lnoff=lnoff: eng.matmul(
                    psu[:, :], wuap[:, hb * 128:(hb + 1) * 128],
                    A2[:, lnoff + hb * T:lnoff + (hb + 1) * T],
                    start=(hb == 0), stop=(hb == NM - 1)),
                    waits=[(lg, nu), (ve, ve.n)] if hb == 0 else (),
                    inc=pe if hb == NM - 1 else None)
            SL.done(su0, ku, pe.n)
            pe_u = pe.n
            gt = A3s(4 + jb % 2)
            ut = A3s(6 + jb % 2)
            S.on("scalar", lambda eng, jb=jb, psg=psg, gt=gt: eng.activation(
                gt, psg[:, :], AF.Silu, scale=CT[:, O_SG + jb:O_SG + jb + 1]),
                waits=[(pe, pe_g), (ve, ve.n)], inc=se)
            se_g = se.n
            S.on("vector", lambda eng, jb=jb, psu=psu, ut=ut: eng.tensor_scalar(
                ut, psu[:, :], CT[:, O_SU + jb:O_SU + jb + 1], None, ALU.mult),
                waits=[(pe, pe_u)], inc=ve)
            fw = list(f_prewait) if jb == 0 else []
            S.on("vector", lambda eng, jb=jb, gt=gt, ut=ut: eng.tensor_tensor(
                A1[:, jb * T:(jb + 1) * T], gt, ut, ALU.mult),
                waits=[(se, se_g)] + fw, inc=ve)
        ve_f = ve.n
        for mb in range(NM):
            pe_cnt = None
            for h2 in range(2):
                s0, k2, wdap, wd_w = SL.get(2)
                nd = S.dma("gpsimd", wdap, wd_in[:, mb * 8192 + h2 * 4096:mb * 8192 + (h2 + 1) * 4096],
                           lg, waits=wd_w)
                for q in range(32):
                    jb = h2 * 32 + q
                    S.on("tensor", lambda eng, q=q, jb=jb, wdap=wdap: eng.matmul(
                        PS[4][:, :], wdap[:, q * 128:(q + 1) * 128],
                        A1[:, jb * T:(jb + 1) * T],
                        start=(jb == 0), stop=(jb == NJ - 1)),
                        waits=[(lg, nd), (ve, max(ve_f, ve.n))] if q == 0 else (),
                        inc=pe if q == 31 else None)
                SL.done(s0, k2, pe.n)
            pe_d = pe.n
            S.on("vector", lambda eng, mb=mb: eng.scalar_tensor_tensor(
                A4[:, mb * T:(mb + 1) * T], PS[4][:, :], CT[:, O_SD + mb:O_SD + mb + 1],
                A4[:, mb * T:(mb + 1) * T], ALU.mult, ALU.add),
                waits=[(pe, pe_d)], inc=ve)
        return ve.n

    def emit_proj(lnoff, wsrc, nmb, psA, psB, bias_off, out_emit, cast=False,
                  mm_waits=()):
        for mb in range(nmb):
            s0, k1, wap, w_w = SL.get(1)
            nw = S.dma("gpsimd" if cast else "sync",
                       wap, wsrc[:, mb * 2048:(mb + 1) * 2048], lg if cast else ld,
                       waits=w_w + ([(pc, lg_wo2)] if mb == 0 else []))
            ps = psA if mb % 2 == 0 else psB
            for hb in range(NM):
                S.on("tensor", lambda eng, hb=hb, wap=wap, ps=ps, lnoff=lnoff: eng.matmul(
                    ps[:, :], wap[:, hb * 128:(hb + 1) * 128],
                    A2[:, lnoff + hb * T:lnoff + (hb + 1) * T],
                    start=(hb == 0), stop=(hb == NM - 1)),
                    waits=([((lg if cast else ld), nw), (ve, ve.n), (se, se.n)]
                           + list(mm_waits)) if hb == 0 else (),
                    inc=pe if hb == NM - 1 else None)
            SL.done(s0, k1, pe.n)
            out_emit(mb, ps, pe.n)

    if upto != "mamba":
        for c in range(NP):
            lx2 = load_chunk_A4(x_d, c, extra=[(ve, ve.n), (se, se.n), (sx, sx.n), (po, po.n)])
            lnw2 = emit_ln(lambda mb: A4[:, mb * T:(mb + 1) * T],
                           lambda mb: A2[:, mb * T:(mb + 1) * T], T,
                           pre_waits=[(ld, lx2), (pe, pe.n), (po, po.n), (ve, ve.n)],
                           sq_scratch=lambda par: A3s(2 + par),
                           cast_scratch=lambda par: A3s(par))
            emit_ffn_gud(0, f_prewait=[(pe, pe.n)])
            for mb in range(NM):
                S.dma("sync", x_d[c, mb, :, :], A4[:, mb * T:(mb + 1) * T], sx,
                      waits=[(ve, ve.n)])
            lnw3 = emit_ln(lambda mb: A4[:, mb * T:(mb + 1) * T],
                           lambda mb: A2[:, 8192 + mb * T:8192 + (mb + 1) * T], T,
                           pre_waits=[(pe, pe.n), (ve, ve.n)],
                           sq_scratch=lambda par: A3s(2 + par),
                           cast_scratch=lambda par: A3s(par))

            def q_emit(mb, ps, pen, c=c):
                qt = A3s(4 + mb % 2)
                S.on("vector", lambda eng, mb=mb, ps=ps, qt=qt: eng.tensor_scalar(
                    qt, ps[:, :], CT[:, O_BQ + mb:O_BQ + mb + 1], None, ALU.add),
                    waits=[(pe, pen), (sx, sx.n)], inc=ve)
                S.dma("sync", q_d[c, mb, :, :], qt, sx, waits=[(ve, ve.n)])
            emit_proj(8192, wqb, 16, PS[5], PS[6], O_BQ, q_emit)

            def k_emit(mb, ps, pen, c=c):
                kt = A3s(4 + mb % 2)
                S.on("vector", lambda eng, mb=mb, ps=ps, kt=kt: eng.tensor_scalar(
                    kt, ps[:, :], CT[:, O_BK + mb:O_BK + mb + 1], None, ALU.add),
                    waits=[(pe, pen), (sx, sx.n)], inc=ve)
                S.dma("sync", cc_i[mb * 128:(mb + 1) * 128, c * T:(c + 1) * T], kt, sx,
                      waits=[(ve, ve.n)])
            emit_proj(8192, wkb, 4, PS[5], PS[6], O_BK, k_emit)

            def v_emit(mb, ps, pen, c=c):
                vt = A3s(4 + mb % 2)
                S.on("vector", lambda eng, mb=mb, ps=ps, vt=vt: eng.tensor_scalar(
                    vt, ps[:, :], CT[:, O_BV + mb:O_BV + mb + 1], None, ALU.add),
                    waits=[(pe, pen), (sx, sx.n)], inc=ve)
                S.dma("sync", cc_i[512 + mb * 128:512 + (mb + 1) * 128, c * T:(c + 1) * T],
                      vt, sx, waits=[(ve, ve.n)])
            emit_proj(8192, wvb, 4, PS[5], PS[6], O_BV, v_emit)

        S.on("gpsimd", lambda eng: eng.collective_compute(
            "AllGather", ALU.bypass,
            replica_groups=[[0, 1], [2, 3], [4, 5], [6, 7]],
            ins=[cc_i[:, :].opt()], outs=[cc_o[:, :].opt()]),
            waits=[(sx, sx.n), (lg, lg.n)], inc=cc, inc_n=1)

        for kv in range(NKV):
            S.dma("sync", A1[:, kv * 4096:kv * 4096 + 2048],
                  cc_o[kv * 128:(kv + 1) * 128, :], ld,
                  waits=[(cc, 1), (pe, pe.n), (ve, ve.n), (se, se.n)] if kv == 0 else ())
            S.dma("sync", A1[:, kv * 4096 + 2048:(kv + 1) * 4096],
                  cc_o[1024 + kv * 128:1024 + (kv + 1) * 128, :], ld)
        for kv in range(NKV):
            l1 = S.dma("sync", A3[:, 0:2048], cc_o[512 + kv * 128:512 + (kv + 1) * 128, :],
                       ld, waits=[(ve, ve.n), (se, se.n), (pe, pe.n), (sx, sx.n)])
            l2 = S.dma("sync", A3[:, 2048:4096],
                       cc_o[1024 + 512 + kv * 128:1024 + 512 + (kv + 1) * 128, :], ld)
            for kt in range(32):
                S.on("tensor", lambda eng, kt=kt: eng.transpose(
                    PSB[:, 0:128], A3[:, kt * 128:(kt + 1) * 128], IDB[:, :]),
                    waits=[(ld, l2), (ve, ve.n)], inc=pe)
                S.on("vector", lambda eng, kv=kv, kt=kt: eng.tensor_copy(
                    A1[:, 16384 + kv * 4096 + kt * 128:16384 + kv * 4096 + (kt + 1) * 128],
                    PSB[:, 0:128]),
                    waits=[(pe, pe.n)], inc=ve)

        qk_free_se = [se.n] * 4
        at_free_pe = [pe.n] * 4
        head_done_ve = ve.n
        STAG = 3
        for qb in range(NP):
            lq = S.dma("sync", A2[:, 0:8192].rearrange("p (mb t) -> p mb t", mb=NM),
                       q_d[qb].transpose([1, 0, 2]), ld,
                       waits=[(pe, pe.n), (ve, ve.n), (sx, sx.n)])
            for h in range(NH):
                kv = h // 4
                QH = A2[:, h * T:(h + 1) * T]
                se_at = [0] * 32
                for step in range(32 + STAG):
                    if step < 32:
                        kt = step
                        psq = PS[kt % 4]
                        w = [(se, qk_free_se[kt % 4])]
                        if h == 0 and kt == 0:
                            w.append((ld, lq))
                        pe_qk = S.on("tensor", lambda eng, kv=kv, kt=kt, psq=psq, QH=QH:
                                     eng.matmul(psq[:, :],
                                                A1[:, kv * 4096 + kt * 128:kv * 4096 + (kt + 1) * 128],
                                                QH, start=True, stop=True),
                                     waits=w, inc=pe)
                        AT = A3s(kt % 4)
                        S.on("scalar", lambda eng, psq=psq, AT=AT: eng.activation(
                            AT, psq[:, :], AF.Exp, scale=CT[:, O_RSQ:O_RSQ + 1]),
                            waits=[(pe, pe_qk), (pe, at_free_pe[kt % 4])], inc=se)
                        se_at[kt] = se.n
                        qk_free_se[kt % 4] = se_at[kt]
                    if step >= STAG:
                        kt = step - STAG
                        AT = A3s(kt % 4)
                        w = [(se, se_at[kt])]
                        if kt == 0:
                            w.append((ve, head_done_ve))
                        S.on("tensor", lambda eng, kv=kv, kt=kt, AT=AT: eng.matmul(
                            PS[4][:, :],
                            A1[:, 16384 + kv * 4096 + kt * 128:16384 + kv * 4096 + (kt + 1) * 128],
                            AT, start=(kt == 0), stop=(kt == 31)),
                            waits=w, inc=pe if kt == 31 else None)
                        S.on("tensor", lambda eng, AT=AT: eng.matmul(
                            PS[5][:, :], ONE[:, :], AT, start=(kt == 0), stop=(kt == 31)),
                            inc=pe)
                        at_free_pe[kt % 4] = pe.n
                pe_pv = pe.n
                S.on("vector", lambda eng: eng.reciprocal(A5[:, 0:512], PS[5][:, :]),
                     waits=[(pe, pe_pv)], inc=ve)
                S.on("vector", lambda eng, h=h: eng.tensor_tensor(
                    A2[:, 8192 + h * T:8192 + (h + 1) * T], PS[4][:, :], A5[:, 0:512],
                    ALU.mult), inc=ve)
                head_done_ve = ve.n
            S.dma("sync", pv_d[qb].transpose([1, 0, 2]),
                  A2[:, 8192:16384].rearrange("p (mb t) -> p mb t", mb=NM), sx,
                  waits=[(ve, ve.n)])

        for c in range(NP):
            lx3 = load_chunk_A4(x_d, c, extra=[(ve, ve.n), (se, se.n), (sx, sx.n)])
            lpv = S.dma("sync", A2[:, 8192:16384].rearrange("p (mb t) -> p mb t", mb=NM),
                        pv_d[c].transpose([1, 0, 2]), ld,
                        waits=[(sx, sx.n), (ve, ve.n)])

            def o_emit(mb, ps, pen, lx3=lx3, lpv=lpv):
                S.on("vector", lambda eng, mb=mb, ps=ps: eng.scalar_tensor_tensor(
                    A4[:, mb * T:(mb + 1) * T], ps[:, :], CT[:, O_BO + mb:O_BO + mb + 1],
                    A4[:, mb * T:(mb + 1) * T], ALU.add, ALU.add),
                    waits=[(pe, pen), (ld, max(lx3, lpv))], inc=ve)
            emit_proj(8192, wob2, 16, PS[5], PS[6], O_BO, o_emit,
                      mm_waits=[(ld, lpv)])

            lnw4 = emit_ln(lambda mb: A4[:, mb * T:(mb + 1) * T],
                           lambda mb: A2[:, mb * T:(mb + 1) * T], T,
                           pre_waits=[(pe, pe.n), (ve, ve.n)],
                           sq_scratch=lambda par: A3s(2 + par),
                           cast_scratch=lambda par: A3s(par))
            emit_ffn_gud(0, f_prewait=[(pe, pe.n)])
            for mb in range(NM):
                S.dma("sync", out[mb * 128:(mb + 1) * 128, c * T:(c + 1) * T],
                      A4[:, mb * T:(mb + 1) * T], sx, waits=[(ve, ve.n)])

    S.on("sync", lambda eng: eng.nop(),
         waits=[(sx, sx.n), (sg_, sg_.n), (ld, ld.n)])
    S.on("gpsimd", lambda eng: eng.nop(), waits=[(lg, lg.n), (pc, pc.n)])

    block = e(nc.Block())
    for s_ in S.sems:
        s_.h = e(nc.semaphore(s_.name))
    S.realize(block)
    st.close()
    counts = {s_.name: s_.n for s_ in S.sems}
    for k, v in counts.items():
        assert v < 60000, (k, v)
    return nc


def kernel(**inputs):
    in_maps = make_shards(inputs)
    nc = build_program(upto="full")
    res = run_bass_kernel_spmd(nc, in_maps, core_ids=list(range(8)))
    outp = np.empty((B, L, DM), np.float32)
    for c in range(8):
        s, half = c // 2, c % 2
        outp[s, half * TC:(half + 1) * TC, :] = res.results[c]["out"].T
    return outp


# version 10 (final)
# speedup vs baseline: 1.0104x; 1.0104x over previous
import numpy as np
from contextlib import ExitStack

import concourse.bass as bass
import concourse.mybir as mybir
from concourse.bass_utils import run_bass_kernel_spmd

F32 = mybir.dt.float32
BF16 = mybir.dt.bfloat16
I8 = mybir.dt.int8
AF = mybir.ActivationFunctionType
ALU = mybir.AluOpType
AXX = mybir.AxisListType.X

B, L, DM = 4, 4096, 2048
DI, NS, TR, KC = 4096, 16, 128, 4
HI, INTER = 2048, 8192
NH, NKV, HD = 16, 4, 128
TC = 2048
T = 512
NP = TC // T
W = 128
NE = DI // 128
NM = DM // 128
NJ = INTER // 128
EPS = 1e-5
KBOUND = 32.0
RSQD = float(1.0 / np.sqrt(128.0))

ENGINES = ("tensor", "vector", "scalar", "gpsimd", "sync")


class Sem:
    __slots__ = ("name", "n", "h")

    def __init__(self, name):
        self.name = name
        self.n = 0
        self.h = None


class Sched:
    def __init__(self):
        self.ops = {e: [] for e in ENGINES}
        self.sems = []

    def sem(self, name):
        s = Sem(name)
        self.sems.append(s)
        return s

    def on(self, engine, fn, waits=(), inc=None, inc_n=1):
        waits = [(s, v) for (s, v) in waits if v > 0]
        for s, v in waits:
            assert v <= s.n, f"wait on {s.name} for {v} > emitted {s.n}"
        self.ops[engine].append((tuple(waits), fn, inc, inc_n))
        if inc is not None:
            inc.n += inc_n
        return inc.n if inc is not None else None

    def dma(self, engine, out, in_, sem, waits=()):
        self.on(engine, lambda eng, o=out, i=in_: eng.dma_start(out=o, in_=i),
                waits=waits, inc=sem, inc_n=16)
        return sem.n

    def realize(self, block):
        for ename in ENGINES:
            items = self.ops[ename]
            if not items:
                continue

            def body(eng, items=items):
                for waits, fn, inc, inc_n in items:
                    for s, v in waits:
                        eng.wait_ge(s.h, v)
                    instr = fn(eng)
                    if inc is not None:
                        instr.then_inc(inc.h, inc_n)
            getattr(block, ename)(body)



def _tiles4(w):
    A, Bc = w.shape
    na, nb = A // 128, Bc // 128
    return np.ascontiguousarray(
        w.reshape(na, 128, nb, 128).transpose(1, 2, 0, 3)).reshape(128, nb * na * 128)


def _tiles4_T(w):
    Bo, Ai = w.shape
    nb, na = Bo // 128, Ai // 128
    return np.ascontiguousarray(
        w.reshape(nb, 128, na, 128).transpose(3, 0, 2, 1)).reshape(128, nb * na * 128)


def make_shards(inputs):
    x = np.asarray(inputs["L0_attn_a_hidden_states"])
    w_in = np.asarray(inputs["L0_attn_a_in_proj_w"])
    w_in_g = _tiles4_T(w_in[DI:, :])
    w_in_h = _tiles4_T(w_in[:DI, :])
    w_in_arr = np.concatenate([w_in_g, w_in_h], axis=1)
    w_out_arr = _tiles4_T(np.asarray(inputs["L0_attn_a_out_proj_w"]))
    xw = np.asarray(inputs["L0_attn_a_x_proj_w"])
    xw_arr = np.ascontiguousarray(xw.T.reshape(NE, 128, 160).transpose(1, 0, 2)).reshape(128, NE * 160)
    dtw = np.asarray(inputs["L0_attn_a_dt_proj_w"])
    dtw_arr = np.ascontiguousarray(dtw.reshape(NE, 128, 128).transpose(2, 0, 1)).reshape(128, NE * 128)
    wq_arr = _tiles4(np.asarray(inputs["L0_attn_b_wq"]))
    wk_arr = _tiles4(np.asarray(inputs["L0_attn_b_wk"]))
    wv_arr = _tiles4(np.asarray(inputs["L0_attn_b_wv"]))
    wo_arr = _tiles4(np.asarray(inputs["L0_attn_b_wo"]))
    wg_arr = _tiles4(np.asarray(inputs["L0_ffn_w_gate"]))
    wu_arr = _tiles4(np.asarray(inputs["L0_ffn_w_up"]))
    wd_arr = _tiles4(np.asarray(inputs["L0_ffn_w_down"]))

    def col_d(v):
        return np.ascontiguousarray(np.asarray(v).reshape(-1, 128).T)

    A_log = np.asarray(inputs["L0_attn_a_A_log"])
    a_arr = np.ascontiguousarray(A_log.reshape(NE, 128, NS).transpose(1, 0, 2)).reshape(128, NE * NS)
    convw = np.asarray(inputs["L0_attn_a_conv_w"])
    convw_arr = np.ascontiguousarray(convw.reshape(NE, 128, KC).transpose(1, 0, 2)).reshape(128, NE * KC)
    consts = np.concatenate([
        a_arr,
        convw_arr,
        col_d(inputs["L0_attn_a_conv_b"]),
        col_d(inputs["L0_attn_a_dt_proj_b"]),
        col_d(inputs["L0_attn_a_D"]),
        col_d(inputs["L0_ffn_s_gate"]),
        col_d(inputs["L0_ffn_s_up"]),
        col_d(inputs["L0_ffn_s_down"]),
        col_d(inputs["L0_attn_b_bq"]),
        col_d(inputs["L0_attn_b_bk"]),
        col_d(inputs["L0_attn_b_bv"]),
        col_d(inputs["L0_attn_b_bo"]),
        np.full((128, 1), EPS, np.float32),
        np.full((128, 1), RSQD, np.float32),
        np.full((128, 1), KBOUND * np.sqrt(128.0), np.float32),
        np.full((128, 1), 0.0, np.float32),
    ], axis=1).astype(np.float32)

    shared = dict(w_in=w_in_arr, w_out=w_out_arr, xw=xw_arr.astype(np.float32),
                  dtw=dtw_arr.astype(np.float32), wq=wq_arr, wk=wk_arr, wv=wv_arr,
                  wo=wo_arr, wg=wg_arr, wu=wu_arr, wd=wd_arr, consts=consts)

    in_maps = []
    for c in range(8):
        s, half = c // 2, c % 2
        t0 = half * TC
        xs = x[s]
        main = xs[t0:t0 + TC]
        if half == 0:
            halo = np.zeros((3, DM), np.float32)
            warm = np.zeros((W, DM), np.float32)
            wmask = np.zeros((128, 1), np.float32)
        else:
            halo = xs[t0 - W - 3:t0 - W]
            warm = xs[t0 - W:t0]
            wmask = np.ones((128, 1), np.float32)
        xin_fm = np.ascontiguousarray(np.concatenate([halo, warm, main], axis=0).T)
        m = dict(shared)
        m["x_fm"] = xin_fm
        m["wmask"] = wmask
        in_maps.append(m)
    return in_maps



def build_program(upto="full", debug=False):
    nc = bass.Bass("TRN2", target_bir_lowering=False, debug=False)
    S = Sched()
    ap = lambda t: t.ap()

    x_fm = ap(nc.dram_tensor("x_fm", [DM, 3 + W + TC], F32, kind="ExternalInput"))
    wmask = ap(nc.dram_tensor("wmask", [128, 1], F32, kind="ExternalInput"))
    consts_in = ap(nc.dram_tensor("consts", [128, 924], F32, kind="ExternalInput"))
    w_in = ap(nc.dram_tensor("w_in", [128, 64 * 16 * 128], F32, kind="ExternalInput"))
    w_out = ap(nc.dram_tensor("w_out", [128, 16 * 32 * 128], F32, kind="ExternalInput"))
    xw_in = ap(nc.dram_tensor("xw", [128, NE * 160], F32, kind="ExternalInput"))
    dtw_in = ap(nc.dram_tensor("dtw", [128, NE * 128], F32, kind="ExternalInput"))
    wq_in = ap(nc.dram_tensor("wq", [128, 16 * 16 * 128], F32, kind="ExternalInput"))
    wk_in = ap(nc.dram_tensor("wk", [128, 4 * 16 * 128], F32, kind="ExternalInput"))
    wv_in = ap(nc.dram_tensor("wv", [128, 4 * 16 * 128], F32, kind="ExternalInput"))
    wo_in = ap(nc.dram_tensor("wo", [128, 16 * 16 * 128], F32, kind="ExternalInput"))
    wg_in = ap(nc.dram_tensor("wg", [128, 64 * 16 * 128], I8, kind="ExternalInput"))
    wu_in = ap(nc.dram_tensor("wu", [128, 64 * 16 * 128], I8, kind="ExternalInput"))
    wd_in = ap(nc.dram_tensor("wd", [128, 16 * 64 * 128], I8, kind="ExternalInput"))

    out = ap(nc.dram_tensor("out", [DM, TC], F32, kind="ExternalOutput"))

    wib = ap(nc.dram_tensor("wib", [128, 64 * 16 * 128], BF16, kind="Internal"))
    dtwb = ap(nc.dram_tensor("dtwb", [128, 32 * 128], BF16, kind="Internal"))
    xwb = ap(nc.dram_tensor("xwb", [128, 32 * 160], BF16, kind="Internal"))
    wob = ap(nc.dram_tensor("wob", [128, 16 * 32 * 128], BF16, kind="Internal"))
    wqb = ap(nc.dram_tensor("wqb", [128, 16 * 16 * 128], BF16, kind="Internal"))
    wkb = ap(nc.dram_tensor("wkb", [128, 4 * 16 * 128], BF16, kind="Internal"))
    wvb = ap(nc.dram_tensor("wvb", [128, 4 * 16 * 128], BF16, kind="Internal"))
    wob2 = ap(nc.dram_tensor("wob2", [128, 16 * 16 * 128], BF16, kind="Internal"))
    x_d = ap(nc.dram_tensor("x_d", [NP, NM, 128, T], F32, kind="Internal"))
    q_d = ap(nc.dram_tensor("q_d", [NP, NM, 128, T], BF16, kind="Internal"))
    cc_i = ap(nc.dram_tensor("cc_i", [1024, TC], BF16, kind="Internal"))
    cc_o = ap(nc.dram_tensor("cc_o", [2048, TC], BF16, kind="Internal"))
    pv_d = ap(nc.dram_tensor("pv_d", [NP, NM, 128, T], BF16, kind="Internal"))

    dbg = {}
    if debug:
        dbg["x1"] = ap(nc.dram_tensor("dbg_x1", [NP, NM, 128, T], F32, kind="ExternalOutput"))
        dbg["xs"] = ap(nc.dram_tensor("dbg_xs", [NE, 128, T], F32, kind="ExternalOutput"))
        dbg["dt"] = ap(nc.dram_tensor("dbg_dt", [NE, 128, T], F32, kind="ExternalOutput"))
        dbg["ssm"] = ap(nc.dram_tensor("dbg_ssm", [160, T], F32, kind="ExternalOutput"))
        dbg["y"] = ap(nc.dram_tensor("dbg_y", [NE, 128, T], F32, kind="ExternalOutput"))
        dbg["tail"] = ap(nc.dram_tensor("dbg_tail", [2, 128, NE * 3], F32, kind="ExternalOutput"))
        dbg["lnh"] = ap(nc.dram_tensor("dbg_lnh", [128, 48], F32, kind="ExternalOutput"))
        dbg["sg"] = ap(nc.dram_tensor("dbg_sg", [NE, 128, T], F32, kind="ExternalOutput"))
        dbg["yg"] = ap(nc.dram_tensor("dbg_yg", [NE, 128, T], F32, kind="ExternalOutput"))
        dbg["yd"] = ap(nc.dram_tensor("dbg_yd", [NE, 128, T], F32, kind="ExternalOutput"))
        dbg["wln"] = ap(nc.dram_tensor("dbg_wln", [NM, 128, T], F32, kind="ExternalOutput"))
        dbg["wxs"] = ap(nc.dram_tensor("dbg_wxs", [NE, 128, T], F32, kind="ExternalOutput"))

    st = ExitStack()
    e = st.enter_context

    A1 = e(nc.sbuf_tensor("A1", [128, 32768], BF16))
    A2 = e(nc.sbuf_tensor("A2", [128, 16384], BF16))
    A3 = e(nc.sbuf_tensor("A3", [128, 8192], BF16))
    A4 = e(nc.sbuf_tensor("A4", [128, 8192], F32))
    SCR = e(nc.sbuf_tensor("SCR", [128, 5120], BF16))
    DTB = e(nc.sbuf_tensor("DTB", [128, 1536], F32))
    A5 = e(nc.sbuf_tensor("A5", [128, 2048], F32))
    WS = e(nc.sbuf_tensor("WS", [128, 8192], BF16))
    CT = e(nc.sbuf_tensor("CT", [128, 1024], F32))
    TSB = e(nc.sbuf_tensor("TSB", [128, 512], BF16))
    BCF = e(nc.sbuf_tensor("BCF", [32, 512], BF16))
    SELS = e(nc.sbuf_tensor("SELS", [32, 32 * 128], BF16))
    HCA = e(nc.sbuf_tensor("HCA", [128, 512], F32))
    HSS = e(nc.sbuf_tensor("HSS", [128, 2048], BF16))
    DTW = e(nc.sbuf_tensor("DTW", [128, 2 * 128], BF16))
    XWS = e(nc.sbuf_tensor("XWS", [128, 2 * 160], BF16))
    ONE = e(nc.sbuf_tensor("ONE", [128, 128], BF16))
    IDB = e(nc.sbuf_tensor("IDB", [128, 128], BF16))
    MSK = e(nc.sbuf_tensor("MSK", [128, 1], F32))
    TAIL = e(nc.sbuf_tensor("TAIL", [128, NE * 3], F32))

    PS = [e(nc.psum_tensor(f"PS{i}", [128, 512], F32)) for i in range(7)]
    PSB = e(nc.psum_tensor("PSB", [128, 1024], BF16))

    O_A, O_CW, O_CB, O_DTB, O_D = 0, 512, 640, 672, 704
    O_SG, O_SU, O_SD, O_BQ, O_BK, O_BV, O_BO = 736, 800, 864, 880, 896, 900, 904
    O_EPS, O_RSQ, O_KB = 920, 921, 922

    ld = S.sem("ld")
    lg = S.sem("lg")
    pe = S.sem("pe")
    ve = S.sem("ve")
    se = S.sem("se")
    po = S.sem("po")
    sx = S.sem("sx")
    sg_ = S.sem("sg")
    cc = S.sem("cc")

    class Slots:
        def __init__(self, n=4, width=2048):
            self.n, self.w = n, width
            self.consumed_pe = [0] * n
            self.rr = 0

        def get(self, k=1):
            s0 = self.rr
            if k == 2:
                if s0 % 2:
                    s0 = (s0 + 1) % self.n
                if s0 + 1 >= self.n:
                    s0 = 0
            self.rr = (s0 + k) % self.n
            waits = []
            for s in range(s0, s0 + k):
                if self.consumed_pe[s] > 0:
                    waits.append((pe, self.consumed_pe[s]))
            ap_ = WS[:, s0 * self.w:(s0 + k) * self.w]
            return (s0, k, ap_, waits)

        def done(self, s0, k, pe_count):
            for s in range(s0, s0 + k):
                self.consumed_pe[s] = pe_count
    SL = Slots()

    S.dma("sync", CT[:, 0:924], consts_in[:, :], ld)
    S.dma("sync", MSK[:, :], wmask[:, :], ld)
    ld_consts = ld.n
    S.on("scalar", lambda eng: eng.activation(CT[:, 0:512], CT[:, 0:512], AF.Exp),
         waits=[(ld, ld_consts)], inc=se)
    S.on("vector", lambda eng: eng.tensor_scalar_mul(CT[:, 0:512], CT[:, 0:512], -1.0),
         waits=[(se, se.n)], inc=ve)
    ve_consts = ve.n
    S.on("vector", lambda eng: eng.memset(ONE[:, :], 1.0), inc=ve)
    S.on("vector", lambda eng: eng.memset(HCA[:, :], 0.0), inc=ve)
    S.on("vector", lambda eng: eng.memset(IDB[:, :], 0.0), inc=ve)
    ve_set = ve.n
    S.on("gpsimd", lambda eng: eng.affine_select(out=IDB[:, :], in_=IDB[:, :], pattern=[[-1, 128]],
                                                 compare_op=ALU.not_equal, fill=1.0, base=0,
                                                 channel_multiplier=1),
         waits=[(ve, ve_set)], inc=po)
    S.on("vector", lambda eng: eng.memset(SELS[:, :], 0.0), inc=ve)
    ve_sel = ve.n
    S.on("gpsimd", lambda eng: eng.affine_select(out=SELS[:, :].rearrange("k (n m) -> k n m", n=32),
                                                 in_=SELS[:, :].rearrange("k (n m) -> k n m", n=32),
                                                 pattern=[[-1, 32], [0, 128]],
                                                 compare_op=ALU.not_equal, fill=1.0, base=0,
                                                 channel_multiplier=1),
         waits=[(ve, ve_sel)], inc=po)
    po_idn = po.n

    def precast(dst, src, total):
        n = 0
        while n < total:
            m = min(16384, total - n)
            S.dma("gpsimd", dst[:, n:n + m], src[:, n:n + m], lg)
            n += m
        return lg.n
    lg_wib = precast(wib, w_in, 64 * 16 * 128)
    precast(dtwb, dtw_in, 32 * 128)
    lg_dtxw = precast(xwb, xw_in, 32 * 160)
    lg_wob = precast(wob, w_out, 16 * 32 * 128)
    precast(wqb, wq_in, 16 * 16 * 128)
    precast(wkb, wk_in, 4 * 16 * 128)
    lg_wkv = precast(wvb, wv_in, 4 * 16 * 128)
    lg_wo2 = precast(wob2, wo_in, 16 * 16 * 128)

    def emit_ln(src_tile, dst_tile, w, pre_waits=(), nt=NM, sq_scratch=None,
                src_bf16=False, cast_scratch=None):
        for mb in range(nt):
            waits = list(pre_waits) if mb == 0 else []
            if mb >= 2:
                waits.append((pe, pe.n))
            if src_bf16:
                mv = src_tile(mb)
                sq_ap = sq_scratch(mb % 2)
                S.on("vector", lambda eng, d=mv, q=sq_ap: eng.tensor_tensor(q, d, d, ALU.mult),
                     waits=waits, inc=ve)
            else:
                mv = cast_scratch(mb % 2)
                sq_ap = sq_scratch(mb % 2)
                S.on("vector", lambda eng, s=src_tile(mb), d=mv: eng.tensor_copy(d, s),
                     waits=waits, inc=ve)
                S.on("vector", lambda eng, d=mv, q=sq_ap: eng.tensor_tensor(q, d, d, ALU.mult),
                     inc=ve)
            vq = ve.n
            S.on("tensor", lambda eng, d=mv, mb=mb, w=w: eng.matmul(
                PS[4][:, 0:w], ONE[:, :], d, start=(mb == 0), stop=(mb == nt - 1)),
                waits=[(ve, vq)], inc=pe)
            S.on("tensor", lambda eng, q=sq_ap, mb=mb, w=w: eng.matmul(
                PS[5][:, 0:w], ONE[:, :], q, start=(mb == 0), stop=(mb == nt - 1)),
                inc=pe)
        pe_stats = pe.n
        ww = min(w, 512)
        MU = A5[:, 0:ww][:, 0:w]
        RS = A5[:, 512:512 + w]
        VR = A5[:, 1024:1024 + w]
        TMP = A5[:, 1536:1536 + w]
        S.on("vector", lambda eng, w=w: eng.tensor_scalar_mul(MU, PS[4][:, 0:w], 1.0 / DM),
             waits=[(pe, pe_stats)], inc=ve)
        S.on("vector", lambda eng: eng.scalar_tensor_tensor(VR, MU, -1.0, MU, ALU.mult, ALU.mult),
             inc=ve)
        S.on("vector", lambda eng, w=w: eng.scalar_tensor_tensor(VR, PS[5][:, 0:w], 1.0 / DM, VR,
                                                                 ALU.mult, ALU.add), inc=ve)
        ve_var = ve.n
        S.on("scalar", lambda eng: eng.activation(VR, VR, AF.Sqrt,
                                                  bias=CT[:, O_EPS:O_EPS + 1]),
             waits=[(ve, ve_var)], inc=se)
        S.on("vector", lambda eng: eng.reciprocal(RS, VR), waits=[(se, se.n)], inc=ve)
        for mb in range(nt):
            S.on("vector", lambda eng, s=src_tile(mb): eng.tensor_tensor(TMP, s, MU, ALU.subtract),
                 inc=ve)
            S.on("vector", lambda eng, d=dst_tile(mb): eng.tensor_tensor(d, TMP, RS, ALU.mult),
                 inc=ve)
        return ve.n

    def XS(i):
        return A1[:, i * T:(i + 1) * T]

    def SG(i):
        return A1[:, 16384 + i * T:16384 + (i + 1) * T]

    def ln_src(mb):
        return A3[:, mb * T:(mb + 1) * T]

    def ln_sq(par):
        return A2[:, 8192 + par * T:8192 + (par + 1) * T]

    def emit_halo_tails():
        for mb in range(NM):
            S.dma("gpsimd", ln_src(mb), x_fm[mb * 128:(mb + 1) * 128, 0:T], lg)
        lw = lg.n
        lnw = emit_ln(ln_src, lambda mb: A2[:, mb * T:(mb + 1) * T], T,
                      pre_waits=[(lg, lw), (ve, ve_consts), (po, po_idn)],
                      sq_scratch=ln_sq, src_bf16=True)
        for i in range(NE):
            s0, k, wap, swaits = SL.get(1)
            nw = S.dma("gpsimd", wap, wib[:, (32 + i) * 2048:(33 + i) * 2048], lg,
                       waits=swaits + ([(lg, lg_wib)] if i == 0 else []))
            for hb in range(NM):
                S.on("tensor", lambda eng, hb=hb, wap=wap: eng.matmul(
                    PS[6][:, 0:3], wap[:, hb * 128:(hb + 1) * 128],
                    A2[:, hb * T:hb * T + 3], start=(hb == 0), stop=(hb == NM - 1)),
                    waits=[(lg, nw), (ve, max(lnw, ve.n))] if hb == 0 else (),
                    inc=pe if hb == NM - 1 else None)
            SL.done(s0, k, pe.n)
            S.on("vector", lambda eng, i=i: eng.tensor_copy(TAIL[:, i * 3:(i + 1) * 3],
                                                            PS[6][:, 0:3]),
                 waits=[(pe, pe.n)], inc=ve)

    def mamba_head(p, wd=T):
        warm = (p < 0)
        st0 = mamba_pass.state
        col0 = 3 + (0 if warm else W + T * p)
        for mb in range(NM):
            S.dma("gpsimd", ln_src(mb)[:, 0:wd],
                  x_fm[mb * 128:(mb + 1) * 128, col0:col0 + wd], lg,
                  waits=[(ve, st0["ve_y_prev"]), (po, st0["po_endscan"])] if mb == 0 else ())
        st0["lx"] = lg.n

    def mamba_pass(p, wd=T):
        warm = (p < 0)
        st0 = mamba_pass.state
        lx = st0["lx"]
        lnw = emit_ln(lambda mb: ln_src(mb)[:, 0:wd],
                      lambda mb: A2[:, mb * T:mb * T + wd], wd,
                      pre_waits=[(lg, lx), (ve, ve_consts), (po, po.n)],
                      sq_scratch=lambda par: ln_sq(par)[:, 0:wd], src_bf16=True)
        if debug and warm:
            for mb in range(NM):
                S.dma("gpsimd", dbg["wln"][mb], A2[:, mb * T:(mb + 1) * T], sg_,
                      waits=[(ve, lnw)])

        def etile(idx_w, out_emit):
            s0, k, wap, swaits = SL.get(1)
            nw = S.dma("gpsimd", wap, wib[:, idx_w * 2048:(idx_w + 1) * 2048], lg,
                       waits=swaits)
            ps = PS[etile.cnt % 2]
            etile.cnt += 1
            for hb in range(NM):
                S.on("tensor", lambda eng, hb=hb, wap=wap, ps=ps: eng.matmul(
                    ps[:, 0:wd], wap[:, hb * 128:(hb + 1) * 128],
                    A2[:, hb * T:hb * T + wd], start=(hb == 0), stop=(hb == NM - 1)),
                    waits=[(lg, nw), (ve, max(lnw, ve.n)), (se, se.n)] if hb == 0 else (),
                    inc=pe if hb == NM - 1 else None)
            SL.done(s0, k, pe.n)
            out_emit(ps, pe.n)
        etile.cnt = 0

        if not warm:
            for i in range(NE):
                def emit_sg(ps, pen, i=i):
                    S.on("scalar", lambda eng: eng.activation(SG(i), ps[:, :], AF.Silu),
                         waits=[(pe, pen)], inc=se)
                etile(i, emit_sg)

        for i in range(NE):
            def emit_conv(ps, pen, i=i):
                HSA = A5[:, 0:wd + 3]
                ACC = A5[:, 515 + (i % 2) * 512:515 + (i % 2) * 512 + wd]
                S.on("vector", lambda eng: eng.tensor_copy(HSA[:, 0:3], TAIL[:, i * 3:(i + 1) * 3]),
                     waits=[(pe, pen)], inc=ve)
                S.on("vector", lambda eng: eng.tensor_copy(HSA[:, 3:3 + wd], ps[:, 0:wd]), inc=ve)
                S.on("vector", lambda eng: eng.tensor_copy(TAIL[:, i * 3:(i + 1) * 3],
                                                           ps[:, wd - 3:wd]), inc=ve)
                ve_hsa = ve.n
                acc_se = mamba_pass.state["acc_se"]
                S.on("vector", lambda eng, ACC=ACC: eng.tensor_scalar(
                    ACC, HSA[:, 0:wd], CT[:, O_CW + i * 4:O_CW + i * 4 + 1], None, ALU.mult),
                    waits=[(ve, ve_hsa)] + ([(se, acc_se[i % 2])]
                                            if acc_se[i % 2] else []), inc=ve)
                for kk in range(1, KC):
                    S.on("vector", lambda eng, kk=kk, ACC=ACC: eng.scalar_tensor_tensor(
                        ACC, HSA[:, kk:kk + wd], CT[:, O_CW + i * 4 + kk:O_CW + i * 4 + kk + 1],
                        ACC, ALU.mult, ALU.add), inc=ve)
                ve_acc = ve.n
                S.on("scalar", lambda eng, ACC=ACC: eng.activation(
                    XS(i)[:, 0:wd], ACC, AF.Silu, bias=CT[:, O_CB + i:O_CB + i + 1]),
                    waits=[(ve, ve_acc)], inc=se)
                mamba_pass.state["acc_se"][i % 2] = se.n
                if debug and warm:
                    S.dma("gpsimd", dbg["wxs"][i], XS(i), sg_, waits=[(se, se.n)])
            etile(32 + i, emit_conv)
        se_xs = se.n

        xw_lg = [0] * NE
        xw_lg[0] = S.dma("sync", XWS[:, 0:160], xwb[:, 0:160], ld,
                         waits=[(lg, lg_dtxw)])
        xw_lg[1] = S.dma("sync", XWS[:, 160:320], xwb[:, 160:320], ld)
        for db in range(NE):
            sl = (db % 2) * 160
            S.on("tensor", lambda eng, db=db, sl=sl: eng.matmul(
                PS[2][:, 0:wd], XWS[:, sl:sl + 128], XS(db)[:, 0:wd],
                start=(db == 0), stop=(db == NE - 1)),
                waits=[(ld, xw_lg[db])] + ([(se, se_xs), (ve, ve.n)] if db == 0 else []),
                inc=pe)
            S.on("tensor", lambda eng, db=db, sl=sl: eng.matmul(
                PS[3][0:32, 0:wd], XWS[:, sl + 128:sl + 160], XS(db)[:, 0:wd],
                start=(db == 0), stop=(db == NE - 1)),
                inc=pe)
            if db + 2 < NE:
                xw_lg[db + 2] = S.dma(
                    "sync", XWS[:, sl:sl + 160],
                    xwb[:, (db + 2) * 160:(db + 3) * 160], ld,
                    waits=[(pe, pe.n)])
        pe_xp = pe.n
        S.on("vector", lambda eng: eng.tensor_copy(TSB[:, 0:wd], PS[2][:, 0:wd]),
             waits=[(pe, pe_xp)], inc=ve)
        S.on("vector", lambda eng: eng.tensor_copy(BCF[:, 0:wd], PS[3][0:32, 0:wd]), inc=ve)
        ve_ts = ve.n
        if debug and p == 0:
            S.dma("gpsimd", dbg["ssm"][0:128, :], TSB[:, :], sg_, waits=[(ve, ve_ts)])
            S.dma("gpsimd", dbg["ssm"][128:160, :], BCF[:, :], sg_)

        for n in range(32):
            S.on("tensor", lambda eng, n=n: eng.matmul(
                PS[2 + (n % 2)][:, 0:wd], SELS[:, n * 128:(n + 1) * 128], BCF[:, 0:wd],
                start=True, stop=True),
                waits=[(ve, max(ve_ts, ve.n))] if n < 2 else [(ve, ve.n)],
                inc=pe)
            S.on("vector", lambda eng, n=n: eng.tensor_copy(
                A2[:, n * T:n * T + wd], PS[2 + (n % 2)][:, 0:wd]),
                waits=[(pe, pe.n)], inc=ve)
        ve_bc = ve.n

        st_ = mamba_pass.state
        NG = 14
        dt_lg = [0] * NE
        dt_pe = [0] * NE
        for i in range(NE):
            if i == 0:
                dt_lg[0] = S.dma("sync", DTW[:, 0:128], dtwb[:, 0:128], ld,
                                 waits=[(pe, st_["dtw_pe"])])
                dt_lg[1] = S.dma("sync", DTW[:, 128:256], dtwb[:, 128:256], ld)
            S.on("tensor", lambda eng, i=i: eng.matmul(
                PS[6][:, 0:wd], DTW[:, (i % 2) * 128:(i % 2) * 128 + 128], TSB[:, 0:wd],
                start=True, stop=True),
                waits=[(ld, dt_lg[i])]
                + ([(ve, ve_bc), (se, se_xs)] if i == 0 else [(se, st_["dti"])]), inc=pe)
            pe_dt = pe.n
            dt_pe[i] = pe_dt
            if i + 2 < NE:
                dt_lg[i + 2] = S.dma(
                    "sync", DTW[:, (i % 2) * 128:(i % 2) * 128 + 128],
                    dtwb[:, (i + 2) * 128:(i + 3) * 128], ld, waits=[(pe, pe_dt)])
            else:
                st_["dtw_pe"] = pe_dt
            DTI = DTB[:, (i % 2) * 512:(i % 2) * 512 + wd]
            DTX = SCR[:, 4096 + (i % 2) * 512:4096 + (i % 2) * 512 + wd]
            E1 = DTB[:, 1024:1024 + wd]
            S.on("scalar", lambda eng, i=i, E1=E1: eng.activation(
                E1, PS[6][:, 0:wd], AF.Exp, bias=CT[:, O_DTB + i:O_DTB + i + 1]),
                waits=[(pe, pe_dt), (ve, st_["dtx"])]
                + ([(sx, sx.n)] if debug else []), inc=se)
            S.on("scalar", lambda eng, DTI=DTI, E1=E1: eng.activation(DTI, E1, AF.Ln, bias=1.0),
                 inc=se)
            st_["dti"] = se.n
            if debug and p == 0:
                S.dma("sync", dbg["dt"][i, :, :], DTI, sx, waits=[(se, st_["dti"])])
            S.on("vector", lambda eng, i=i, DTX=DTX, DTI=DTI: eng.tensor_tensor(
                DTX, DTI, XS(i)[:, 0:wd], ALU.mult),
                waits=[(se, st_["dti"])], inc=ve)
            if warm:
                S.on("vector", lambda eng, DTX=DTX: eng.tensor_scalar(
                    DTX, DTX, MSK[:, 0:1], None, ALU.mult), inc=ve)
            st_["dtx"] = ve.n
            for n in range(NG, NS):
                g = n - NG
                UBG = SCR[:, 3072 + g * 512:3072 + g * 512 + wd]
                S.on("gpsimd", lambda eng, n=n, UBG=UBG, DTX=DTX: eng.tensor_tensor(
                    UBG, DTX, A2[:, n * T:n * T + wd], ALU.mult),
                    waits=[(ve, st_["dtx"]), (ve, st_["ubg_free"][g])], inc=po)
                st_["ub_g"][g] = po.n
            po_m7 = 0
            for n in range(NS):
                j = i * NS + n
                sda = j % 4
                DA = SCR[:, sda * 512:sda * 512 + wd]
                if n < NG:
                    UB = SCR[:, 2048 + (j % 2) * 512:2048 + (j % 2) * 512 + wd]
                else:
                    UB = SCR[:, 3072 + (n - NG) * 512:3072 + (n - NG) * 512 + wd]
                HS_ = HSS[:, sda * 512:sda * 512 + wd]
                S.on("scalar", lambda eng, j=j, DA=DA, DTI=DTI: eng.activation(
                    DA, DTI, AF.Exp, scale=CT[:, O_A + j:O_A + j + 1]),
                    waits=[(ve, st_["da_free"][sda])], inc=se)
                se_da = se.n
                if n < NG:
                    S.on("vector", lambda eng, n=n, UB=UB, DTX=DTX: eng.tensor_tensor(
                        UB, DTX, A2[:, n * T:n * T + wd], ALU.mult), inc=ve)
                sw = [(se, se_da)]
                if n >= NG:
                    sw.append((po, st_["ub_g"][n - NG]))
                if not warm:
                    sw.append((po, st_["hs_free"][sda]))
                S.on("vector", lambda eng, j=j, HS_=HS_, DA=DA, UB=UB: eng.tensor_tensor_scan(
                    HS_, DA, UB, HCA[:, j:j + 1], ALU.mult, ALU.add),
                    waits=sw, inc=ve)
                ve_scan = ve.n
                st_["da_free"][sda] = ve_scan
                st_["scan_ve"][sda] = ve_scan
                if n < NG and n % 2 == 0:
                    st_["ubv0_scan"] = ve_scan
                if n >= NG:
                    st_["ubg_free"][n - NG] = ve_scan
                if n >= 3:
                    jc = j - 3
                    S.on("scalar", lambda eng, jc=jc: eng.copy(
                        HCA[:, jc:jc + 1],
                        HSS[:, (jc % 4) * 512 + wd - 1:(jc % 4) * 512 + wd]),
                        waits=[(ve, st_["scan_ve"][jc % 4])], inc=se)
                if not warm:
                    S.on("gpsimd", lambda eng, n=n, HS_=HS_: eng.tensor_tensor(
                        A3[:, n * T:(n + 1) * T], HS_, A2[:, (16 + n) * T:(17 + n) * T], ALU.mult),
                        waits=[(ve, ve_scan)], inc=po)
                    st_["hs_free"][sda] = po.n
                    if n == 7:
                        po_m7 = po.n
            for jc in range(i * NS + NS - 3, i * NS + NS):
                S.on("scalar", lambda eng, jc=jc: eng.copy(
                    HCA[:, jc:jc + 1],
                    HSS[:, (jc % 4) * 512 + wd - 1:(jc % 4) * 512 + wd]),
                    waits=[(ve, st_["scan_ve"][jc % 4])], inc=se)
            if warm:
                continue
            for (a, b) in ((8, 12), (9, 13), (10, 14), (11, 15), (8, 10), (9, 11), (8, 9)):
                S.on("gpsimd", lambda eng, a=a, b=b: eng.tensor_tensor(
                    A3[:, a * T:(a + 1) * T], A3[:, a * T:(a + 1) * T],
                    A3[:, b * T:(b + 1) * T], ALU.add), inc=po)
            po_tree = po.n
            YI = A5[:, 0:512]
            hv = A3[:, 0:8 * T].rearrange("p (n t) -> p t n", n=8)
            S.on("vector", lambda eng, YI=YI, hv=hv: eng.tensor_reduce(YI, hv, AXX, ALU.add),
                 waits=[(po, po_m7)], inc=ve)
            S.on("vector", lambda eng, YI=YI: eng.tensor_tensor(
                YI, YI, A3[:, 8 * T:9 * T], ALU.add), waits=[(po, po_tree)], inc=ve)
            dwait = []
            if debug and p == 0:
                S.dma("sync", dbg["y"][i, :, :], YI, sx, waits=[(ve, ve.n)])
                dwait = [(sx, sx.n)]
            S.on("vector", lambda eng, i=i: eng.scalar_tensor_tensor(
                YI, XS(i), CT[:, O_D + i:O_D + i + 1], YI, ALU.mult, ALU.add),
                waits=dwait, inc=ve)
            dwait = []
            if debug and p == 0:
                S.dma("sync", dbg["yd"][i, :, :], YI, sx, waits=[(ve, ve.n)])
                S.dma("gpsimd", dbg["xs"][i, :, :], XS(i), sg_, waits=[(ve, ve.n)])
                S.dma("gpsimd", dbg["sg"][i, :, :], SG(i), sg_)
                dwait = [(sg_, sg_.n), (sx, sx.n)]
            S.on("vector", lambda eng, i=i: eng.tensor_tensor(XS(i), YI, SG(i), ALU.mult),
                 waits=dwait, inc=ve)
            if debug and p == 0:
                S.dma("gpsimd", dbg["yg"][i, :, :], XS(i), sg_, waits=[(ve, ve.n)])
        st_["ve_y_prev"] = ve.n
        st_["po_endscan"] = po.n
        st_["ve_y"] = ve.n

    def mamba_out(p):
        st0 = mamba_pass.state
        ve_y = st0["ve_y"]
        col0 = 3 + W + T * p
        for mb in range(NM):
            s0, k, wap, swaits = SL.get(2)
            nw = S.dma("gpsimd", wap, wob[:, mb * 4096:(mb + 1) * 4096], lg,
                       waits=swaits + ([(lg, lg_wob)] if mb == 0 else []))
            ps = PS[mb % 2]
            for db in range(NE):
                S.on("tensor", lambda eng, db=db, wap=wap, ps=ps: eng.matmul(
                    ps[:, :], wap[:, db * 128:(db + 1) * 128], XS(db),
                    start=(db == 0), stop=(db == NE - 1)),
                    waits=[(lg, nw), (ve, max(ve_y, ve.n))] if db == 0 else (),
                    inc=pe if db == NE - 1 else None)
            SL.done(s0, k, pe.n)
            pe_o = pe.n
            xa = A4[:, 6144 + (mb % 4) * T:6144 + (mb % 4) * T + T]
            lxr = S.dma("sync", xa, x_fm[mb * 128:(mb + 1) * 128, col0:col0 + T], ld,
                        waits=([(ve, ve.n), (po, po.n)] if mb < 4 else []) + [(sx, sx.n)])
            S.on("vector", lambda eng, xa=xa, ps=ps: eng.tensor_tensor(xa, xa, ps[:, :], ALU.add),
                 waits=[(pe, pe_o), (ld, lxr)], inc=ve)
            ve_x1 = ve.n
            S.dma("sync", x_d[p, mb, :, :], xa, sx, waits=[(ve, ve_x1)])
            if debug:
                S.dma("sync", dbg["x1"][p, mb, :, :], xa, sx)

    mamba_pass.state = {"dti": 0, "dtx": 0, "da_free": [0, 0, 0, 0],
                        "hs_free": [0, 0, 0, 0], "ubg_free": [0, 0],
                        "ub_g": [0, 0], "acc_se": [0, 0],
                        "scan_ve": [0, 0, 0, 0], "ubv0_scan": 0, "dtw_pe": 0,
                        "ve_y_prev": 0, "po_endscan": 0, "ve_y": 0, "lx": 0}
    emit_halo_tails()
    if debug:
        S.dma("sync", dbg["tail"][0], TAIL[:, :], sx, waits=[(ve, ve.n)])
        S.dma("gpsimd", dbg["lnh"][:, :].rearrange("p (mb j) -> p mb j", j=3), A2[:, 0:NM * T].rearrange("p (mb t) -> p mb t", mb=NM)[:, :, 0:3], sg_, waits=[(ve, ve.n)])
    mamba_pass.state["ve_y_prev"] = ve.n
    mamba_pass.state["po_endscan"] = po.n
    mamba_head(-1, wd=W)
    mamba_pass(-1, wd=W)
    if debug:
        S.dma("sync", dbg["tail"][1], TAIL[:, :], sx, waits=[(ve, ve.n)])
    for p in range(NP):
        mamba_head(p)
        if p > 0:
            mamba_out(p - 1)
        mamba_pass(p)
    mamba_out(NP - 1)

    def A3s(k):
        return A3[:, k * T:(k + 1) * T]

    def load_chunk_A4(dram4, c, extra=()):
        return S.dma("sync", A4[:, 0:NM * T].rearrange("p (mb t) -> p mb t", mb=NM),
                     dram4[c].transpose([1, 0, 2]), ld, waits=extra)

    def emit_ffn_gud(lnoff, f_prewait=()):
        res = []
        for jb in range(NJ):
            sg0, kg, wgap, wg_w = SL.get(1)
            ng = S.dma("gpsimd", wgap, wg_in[:, jb * 2048:(jb + 1) * 2048], lg, waits=wg_w)
            su0, ku, wuap, wu_w = SL.get(1)
            nu = S.dma("gpsimd", wuap, wu_in[:, jb * 2048:(jb + 1) * 2048], lg, waits=wu_w)
            psg = PS[jb % 2]
            psu = PS[2 + jb % 2]
            for hb in range(NM):
                S.on("tensor", lambda eng, hb=hb, wgap=wgap, psg=psg, lnoff=lnoff: eng.matmul(
                    psg[:, :], wgap[:, hb * 128:(hb + 1) * 128],
                    A2[:, lnoff + hb * T:lnoff + (hb + 1) * T],
                    start=(hb == 0), stop=(hb == NM - 1)),
                    waits=[(lg, ng), (se, se.n), (ve, ve.n)] if hb == 0 else (),
                    inc=pe if hb == NM - 1 else None)
            SL.done(sg0, kg, pe.n)
            pe_g = pe.n
            for hb in range(NM):
                S.on("tensor", lambda eng, hb=hb, wuap=wuap, psu=psu, lnoff=lnoff: eng.matmul(
                    psu[:, :], wuap[:, hb * 128:(hb + 1) * 128],
                    A2[:, lnoff + hb * T:lnoff + (hb + 1) * T],
                    start=(hb == 0), stop=(hb == NM - 1)),
                    waits=[(lg, nu), (ve, ve.n)] if hb == 0 else (),
                    inc=pe if hb == NM - 1 else None)
            SL.done(su0, ku, pe.n)
            pe_u = pe.n
            gt = A3s(4 + jb % 2)
            ut = A3s(6 + jb % 2)
            S.on("scalar", lambda eng, jb=jb, psg=psg, gt=gt: eng.activation(
                gt, psg[:, :], AF.Silu, scale=CT[:, O_SG + jb:O_SG + jb + 1]),
                waits=[(pe, pe_g), (ve, ve.n)], inc=se)
            se_g = se.n
            S.on("vector", lambda eng, jb=jb, psu=psu, ut=ut: eng.tensor_scalar(
                ut, psu[:, :], CT[:, O_SU + jb:O_SU + jb + 1], None, ALU.mult),
                waits=[(pe, pe_u)], inc=ve)
            fw = list(f_prewait) if jb == 0 else []
            S.on("vector", lambda eng, jb=jb, gt=gt, ut=ut: eng.tensor_tensor(
                A1[:, jb * T:(jb + 1) * T], gt, ut, ALU.mult),
                waits=[(se, se_g)] + fw, inc=ve)
        ve_f = ve.n
        for mb in range(NM):
            pe_cnt = None
            for h2 in range(2):
                s0, k2, wdap, wd_w = SL.get(2)
                nd = S.dma("gpsimd", wdap, wd_in[:, mb * 8192 + h2 * 4096:mb * 8192 + (h2 + 1) * 4096],
                           lg, waits=wd_w)
                for q in range(32):
                    jb = h2 * 32 + q
                    S.on("tensor", lambda eng, q=q, jb=jb, wdap=wdap: eng.matmul(
                        PS[4][:, :], wdap[:, q * 128:(q + 1) * 128],
                        A1[:, jb * T:(jb + 1) * T],
                        start=(jb == 0), stop=(jb == NJ - 1)),
                        waits=[(lg, nd), (ve, max(ve_f, ve.n))] if q == 0 else (),
                        inc=pe if q == 31 else None)
                SL.done(s0, k2, pe.n)
            pe_d = pe.n
            S.on("vector", lambda eng, mb=mb: eng.scalar_tensor_tensor(
                A4[:, mb * T:(mb + 1) * T], PS[4][:, :], CT[:, O_SD + mb:O_SD + mb + 1],
                A4[:, mb * T:(mb + 1) * T], ALU.mult, ALU.add),
                waits=[(pe, pe_d)], inc=ve)
        return ve.n

    def emit_proj(lnoff, wsrc, nmb, psA, psB, bias_off, out_emit, cast=False,
                  mm_waits=()):
        for mb in range(nmb):
            s0, k1, wap, w_w = SL.get(1)
            nw = S.dma("gpsimd" if cast else "sync",
                       wap, wsrc[:, mb * 2048:(mb + 1) * 2048], lg if cast else ld,
                       waits=w_w)
            ps = psA if mb % 2 == 0 else psB
            for hb in range(NM):
                S.on("tensor", lambda eng, hb=hb, wap=wap, ps=ps, lnoff=lnoff: eng.matmul(
                    ps[:, :], wap[:, hb * 128:(hb + 1) * 128],
                    A2[:, lnoff + hb * T:lnoff + (hb + 1) * T],
                    start=(hb == 0), stop=(hb == NM - 1)),
                    waits=([((lg if cast else ld), nw), (ve, ve.n), (se, se.n)]
                           + list(mm_waits)) if hb == 0 else (),
                    inc=pe if hb == NM - 1 else None)
            SL.done(s0, k1, pe.n)
            out_emit(mb, ps, pe.n)

    if upto != "mamba":
        for c in range(NP):
            lx2 = load_chunk_A4(x_d, c, extra=[(ve, ve.n), (se, se.n), (sx, sx.n), (po, po.n)])
            lnw2 = emit_ln(lambda mb: A4[:, mb * T:(mb + 1) * T],
                           lambda mb: A2[:, mb * T:(mb + 1) * T], T,
                           pre_waits=[(ld, lx2), (pe, pe.n), (po, po.n), (ve, ve.n)],
                           sq_scratch=lambda par: A3s(2 + par),
                           cast_scratch=lambda par: A3s(par))
            emit_ffn_gud(0, f_prewait=[(pe, pe.n)])
            for mb in range(NM):
                S.dma("sync", x_d[c, mb, :, :], A4[:, mb * T:(mb + 1) * T], sx,
                      waits=[(ve, ve.n)])
            lnw3 = emit_ln(lambda mb: A4[:, mb * T:(mb + 1) * T],
                           lambda mb: A2[:, 8192 + mb * T:8192 + (mb + 1) * T], T,
                           pre_waits=[(pe, pe.n), (ve, ve.n)],
                           sq_scratch=lambda par: A3s(2 + par),
                           cast_scratch=lambda par: A3s(par))

            def q_emit(mb, ps, pen, c=c):
                qt = A3s(4 + mb % 2)
                S.on("vector", lambda eng, mb=mb, ps=ps, qt=qt: eng.tensor_scalar(
                    qt, ps[:, :], CT[:, O_BQ + mb:O_BQ + mb + 1], None, ALU.add),
                    waits=[(pe, pen), (sx, sx.n)], inc=ve)
                S.dma("sync", q_d[c, mb, :, :], qt, sx, waits=[(ve, ve.n)])
            emit_proj(8192, wqb, 16, PS[5], PS[6], O_BQ, q_emit)

            def k_emit(mb, ps, pen, c=c):
                kt = A3s(4 + mb % 2)
                S.on("vector", lambda eng, mb=mb, ps=ps, kt=kt: eng.tensor_scalar(
                    kt, ps[:, :], CT[:, O_BK + mb:O_BK + mb + 1], None, ALU.add),
                    waits=[(pe, pen), (sx, sx.n)], inc=ve)
                S.dma("sync", cc_i[mb * 128:(mb + 1) * 128, c * T:(c + 1) * T], kt, sx,
                      waits=[(ve, ve.n)])
            emit_proj(8192, wkb, 4, PS[5], PS[6], O_BK, k_emit)

            def v_emit(mb, ps, pen, c=c):
                vt = A3s(4 + mb % 2)
                S.on("vector", lambda eng, mb=mb, ps=ps, vt=vt: eng.tensor_scalar(
                    vt, ps[:, :], CT[:, O_BV + mb:O_BV + mb + 1], None, ALU.add),
                    waits=[(pe, pen), (sx, sx.n)], inc=ve)
                S.dma("sync", cc_i[512 + mb * 128:512 + (mb + 1) * 128, c * T:(c + 1) * T],
                      vt, sx, waits=[(ve, ve.n)])
            emit_proj(8192, wvb, 4, PS[5], PS[6], O_BV, v_emit)

        S.on("gpsimd", lambda eng: eng.collective_compute(
            "AllGather", ALU.bypass,
            replica_groups=[[0, 1], [2, 3], [4, 5], [6, 7]],
            ins=[cc_i[:, :].opt()], outs=[cc_o[:, :].opt()]),
            waits=[(sx, sx.n), (lg, lg.n)], inc=cc, inc_n=1)

        for kv in range(NKV):
            S.dma("sync", A1[:, kv * 4096:kv * 4096 + 2048],
                  cc_o[kv * 128:(kv + 1) * 128, :], ld,
                  waits=[(cc, 1), (pe, pe.n), (ve, ve.n), (se, se.n)] if kv == 0 else ())
            S.dma("sync", A1[:, kv * 4096 + 2048:(kv + 1) * 4096],
                  cc_o[1024 + kv * 128:1024 + (kv + 1) * 128, :], ld)
        for kv in range(NKV):
            l1 = S.dma("sync", A3[:, 0:2048], cc_o[512 + kv * 128:512 + (kv + 1) * 128, :],
                       ld, waits=[(ve, ve.n), (se, se.n), (pe, pe.n), (sx, sx.n)])
            l2 = S.dma("sync", A3[:, 2048:4096],
                       cc_o[1024 + 512 + kv * 128:1024 + 512 + (kv + 1) * 128, :], ld)
            for kt in range(32):
                S.on("tensor", lambda eng, kt=kt: eng.transpose(
                    PSB[:, 0:128], A3[:, kt * 128:(kt + 1) * 128], IDB[:, :]),
                    waits=[(ld, l2), (ve, ve.n)], inc=pe)
                S.on("vector", lambda eng, kv=kv, kt=kt: eng.tensor_copy(
                    A1[:, 16384 + kv * 4096 + kt * 128:16384 + kv * 4096 + (kt + 1) * 128],
                    PSB[:, 0:128]),
                    waits=[(pe, pe.n)], inc=ve)

        qk_free_se = [se.n] * 4
        at_free_pe = [pe.n] * 4
        head_done_ve = ve.n
        STAG = 3
        for qb in range(NP):
            lq = S.dma("sync", A2[:, 0:8192].rearrange("p (mb t) -> p mb t", mb=NM),
                       q_d[qb].transpose([1, 0, 2]), ld,
                       waits=[(pe, pe.n), (ve, ve.n), (sx, sx.n)])
            for h in range(NH):
                kv = h // 4
                QH = A2[:, h * T:(h + 1) * T]
                se_at = [0] * 32
                for step in range(32 + STAG):
                    if step < 32:
                        kt = step
                        psq = PS[kt % 4]
                        w = [(se, qk_free_se[kt % 4])]
                        if h == 0 and kt == 0:
                            w.append((ld, lq))
                        pe_qk = S.on("tensor", lambda eng, kv=kv, kt=kt, psq=psq, QH=QH:
                                     eng.matmul(psq[:, :],
                                                A1[:, kv * 4096 + kt * 128:kv * 4096 + (kt + 1) * 128],
                                                QH, start=True, stop=True),
                                     waits=w, inc=pe)
                        AT = A3s(kt % 4)
                        S.on("scalar", lambda eng, psq=psq, AT=AT: eng.activation(
                            AT, psq[:, :], AF.Exp, scale=CT[:, O_RSQ:O_RSQ + 1]),
                            waits=[(pe, pe_qk), (pe, at_free_pe[kt % 4])], inc=se)
                        se_at[kt] = se.n
                        qk_free_se[kt % 4] = se_at[kt]
                    if step >= STAG:
                        kt = step - STAG
                        AT = A3s(kt % 4)
                        w = [(se, se_at[kt])]
                        if kt == 0:
                            w.append((ve, head_done_ve))
                        S.on("tensor", lambda eng, kv=kv, kt=kt, AT=AT: eng.matmul(
                            PS[4][:, :],
                            A1[:, 16384 + kv * 4096 + kt * 128:16384 + kv * 4096 + (kt + 1) * 128],
                            AT, start=(kt == 0), stop=(kt == 31)),
                            waits=w, inc=pe if kt == 31 else None)
                        S.on("tensor", lambda eng, AT=AT: eng.matmul(
                            PS[5][:, :], ONE[:, :], AT, start=(kt == 0), stop=(kt == 31)),
                            inc=pe)
                        at_free_pe[kt % 4] = pe.n
                pe_pv = pe.n
                S.on("vector", lambda eng: eng.reciprocal(A5[:, 0:512], PS[5][:, :]),
                     waits=[(pe, pe_pv)], inc=ve)
                S.on("vector", lambda eng, h=h: eng.tensor_tensor(
                    A2[:, 8192 + h * T:8192 + (h + 1) * T], PS[4][:, :], A5[:, 0:512],
                    ALU.mult), inc=ve)
                head_done_ve = ve.n
            S.dma("sync", pv_d[qb].transpose([1, 0, 2]),
                  A2[:, 8192:16384].rearrange("p (mb t) -> p mb t", mb=NM), sx,
                  waits=[(ve, ve.n)])

        for c in range(NP):
            lx3 = load_chunk_A4(x_d, c, extra=[(ve, ve.n), (se, se.n), (sx, sx.n)])
            lpv = S.dma("sync", A2[:, 8192:16384].rearrange("p (mb t) -> p mb t", mb=NM),
                        pv_d[c].transpose([1, 0, 2]), ld,
                        waits=[(sx, sx.n), (ve, ve.n)])

            def o_emit(mb, ps, pen, lx3=lx3, lpv=lpv):
                S.on("vector", lambda eng, mb=mb, ps=ps: eng.scalar_tensor_tensor(
                    A4[:, mb * T:(mb + 1) * T], ps[:, :], CT[:, O_BO + mb:O_BO + mb + 1],
                    A4[:, mb * T:(mb + 1) * T], ALU.add, ALU.add),
                    waits=[(pe, pen), (ld, max(lx3, lpv))], inc=ve)
            emit_proj(8192, wob2, 16, PS[5], PS[6], O_BO, o_emit,
                      mm_waits=[(ld, lpv)])

            lnw4 = emit_ln(lambda mb: A4[:, mb * T:(mb + 1) * T],
                           lambda mb: A2[:, mb * T:(mb + 1) * T], T,
                           pre_waits=[(pe, pe.n), (ve, ve.n)],
                           sq_scratch=lambda par: A3s(2 + par),
                           cast_scratch=lambda par: A3s(par))
            emit_ffn_gud(0, f_prewait=[(pe, pe.n)])
            for mb in range(NM):
                S.dma("sync", out[mb * 128:(mb + 1) * 128, c * T:(c + 1) * T],
                      A4[:, mb * T:(mb + 1) * T], sx, waits=[(ve, ve.n)])

    S.on("sync", lambda eng: eng.nop(),
         waits=[(sx, sx.n), (sg_, sg_.n), (ld, ld.n)])
    S.on("gpsimd", lambda eng: eng.nop(), waits=[(lg, lg.n)])

    block = e(nc.Block())
    for s_ in S.sems:
        s_.h = e(nc.semaphore(s_.name))
    S.realize(block)
    st.close()
    counts = {s_.name: s_.n for s_ in S.sems}
    for k, v in counts.items():
        assert v < 60000, (k, v)
    return nc


def kernel(**inputs):
    in_maps = make_shards(inputs)
    nc = build_program(upto="full")
    res = run_bass_kernel_spmd(nc, in_maps, core_ids=list(range(8)))
    outp = np.empty((B, L, DM), np.float32)
    for c in range(8):
        s, half = c // 2, c % 2
        outp[s, half * TC:(half + 1) * TC, :] = res.results[c]["out"].T
    return outp
